# version 15
# baseline (speedup 1.0000x reference)
import numpy as np
import ml_dtypes
import concourse.bass as bass
import concourse.mybir as mybir
from concourse.bass_utils import run_bass_kernel_spmd
from contextlib import ExitStack

F32 = mybir.dt.float32
BF16 = mybir.dt.bfloat16
I32 = mybir.dt.int32
AF = mybir.ActivationFunctionType
ALU = mybir.AluOpType
AX = mybir.AxisListType

D = 2048
E = 8192
H = 16
F3 = 6144
DC = D // 128
EC = E // 128
ISQ = 0.08838834764831845
LN_EPS = 1e-5
Q_EPS = 1e-8
MAGIC = 0x5F3759DF
AR = 105000

ENGS = ("sync", "pe", "vec", "act", "gp")
NPOOL_S = 14
NPOOL_G = 6
NPOOL_A = 10
POOL = {"sync": ["qS%d" % i for i in range(NPOOL_S)],
        "gp": ["qG%d" % i for i in range(NPOOL_G)],
        "act": ["qA%d" % i for i in range(NPOOL_A)]}
SEMS = (tuple(POOL["sync"]) + tuple(POOL["gp"]) + tuple(POOL["act"])
        + ("tT", "tV", "tS", "tG"))


class Build:
    def __init__(self):
        self.ops = {e: [] for e in ENGS}
        self.cnt = {s: 0 for s in SEMS}
        self.hw = {e: {} for e in ENGS}

    def wait(self, e, tok):
        if tok is None:
            return
        sem, val = tok
        if val <= 0 or self.hw[e].get(sem, 0) >= val:
            return
        self.hw[e][sem] = val
        self.ops[e].append(("w", sem, val))

    def waits(self, e, toks):
        for t in toks:
            self.wait(e, t)

    def emit(self, e, fn, inc=None, k=1):
        tok = None
        if inc is not None:
            self.cnt[inc] += k
            tok = (inc, self.cnt[inc])
        self.ops[e].append(("i", fn, inc, k))
        return tok

    def barrier(self):
        snap = dict(self.cnt)
        for e in ENGS:
            for s, v in snap.items():
                self.wait(e, (s, v))

    def replay(self, block, semh):
        def mk(entries):
            def body(eng):
                for op in entries:
                    if op[0] == "w":
                        eng.wait_ge(semh[op[1]], op[2])
                    else:
                        inst = op[1](eng)
                        if op[2] is not None:
                            inst.then_inc(semh[op[2]], op[3])
            return body
        block.sync(mk(self.ops["sync"]))
        block.tensor(mk(self.ops["pe"]))
        block.vector(mk(self.ops["vec"]))
        block.scalar(mk(self.ops["act"]))
        block.gpsimd(mk(self.ops["gp"]))


def build_nc(TOK, upto=None):
    _seen = []

    def _go(name):
        if upto is not None and upto in _seen:
            return False
        _seen.append(name)
        return True

    NT = TOK // 128
    NTC = TOK // 512
    NBLK = TOK // 512
    NKC = TOK // 512
    NKQ = max(1, TOK // 1024)
    KQW = min(1024, TOK)
    NUQ = max(1, TOK // 1024)
    UW = min(1024, TOK)

    nc = bass.Bass("TRN2", target_bir_lowering=False)

    x_in = nc.dram_tensor("x", [TOK, D], F32, kind="ExternalInput").ap()
    wqkv = nc.dram_tensor("wqkv", [D, F3], F32, kind="ExternalInput").ap()
    wo = nc.dram_tensor("wo", [D, D], F32, kind="ExternalInput").ap()
    wup = nc.dram_tensor("wup", [E, D], F32, kind="ExternalInput").ap()
    wgate = nc.dram_tensor("wgate", [E, D], F32, kind="ExternalInput").ap()
    wdown = nc.dram_tensor("wdown", [D, E], F32, kind="ExternalInput").ap()
    cbf = nc.dram_tensor("cbf", [128, 160], BF16, kind="ExternalInput").ap()
    cf32 = nc.dram_tensor("cf32", [128, 256], F32, kind="ExternalInput").ap()
    ci32 = nc.dram_tensor("ci32", [128, 8], I32, kind="ExternalInput").ap()
    out = nc.dram_tensor("out", [TOK, D], F32, kind="ExternalOutput").ap()

    s_x0 = nc.dram_tensor("s_x0", [DC, 128, TOK], F32).ap()
    s_x1 = nc.dram_tensor("s_x1", [DC, 128, TOK], F32).ap()
    s_x2 = nc.dram_tensor("s_x2", [DC, 128, TOK], F32).ap()
    s_x3 = nc.dram_tensor("s_x3", [DC, 128, TOK], F32).ap()
    s_xln = nc.dram_tensor("s_xln", [NTC, 128, DC * 512], BF16).ap()
    s_tug = nc.dram_tensor("s_tug", [EC, 128, 4096], BF16).ap()
    s_td = nc.dram_tensor("s_td", [DC, 128, EC * 128], BF16).ap()
    s_cv = nc.dram_tensor("s_cv", [NTC, 128, H * 512], BF16).ap()

    B = Build()

    with ExitStack() as ctx:
        sbt = lambda n, s, d: ctx.enter_context(nc.sbuf_tensor(n, s, d))
        c_bf = sbt("c_bf", [128, 160], BF16)
        c_f = sbt("c_f", [128, 256], F32)
        c_i = sbt("c_i", [128, 8], I32)
        sm = sbt("sm", [128, 64], F32)
        arena = sbt("arena", [128, AR], BF16)
        ps = ctx.enter_context(nc.psum_tensor("ps", [128, 4096], F32))
        semh = {n: ctx.enter_context(nc.semaphore(n)) for n in SEMS}
        block = ctx.enter_context(nc.Block())

        ident_b = c_bf[:, 0:128]
        ones_b = c_bf[:, 128:160]
        ident_f = c_f[:, 0:128]
        ones_f = c_f[:, 128:256]

        def av(off, n, dt=BF16, p0=0, p1=128):
            nel = n if dt == BF16 else 2 * n
            assert off + nel <= AR, (off, n, dt)
            if dt == BF16:
                return arena[p0:p1, off:off + n]
            assert off % 2 == 0
            return arena[p0:p1, off:off + 2 * n].bitcast(dt)

        def bank(i, w=1):
            return ps[:, 512 * i:512 * (i + w)]

        SM_ACC = 0
        SM_TMP = 3
        SM_TH = 4
        SM_RS = 10
        SM_Z = 12
        SM_ZS, SM_IZ, SM_BIAS, SM_QM, SM_KM = 36, 39, 42, 43, 44
        SM_SC = 46

        bk_w = [None] * 8
        bk_r = [[] for _ in range(8)]

        def mm_group(banks_used, fns):
            for b in banks_used:
                B.waits("pe", bk_r[b])
            t = None
            for i, f in enumerate(fns):
                t = B.emit("pe", f, inc="tT" if i == len(fns) - 1 else None)
            for b in banks_used:
                bk_w[b] = t
                bk_r[b] = []
            return t

        def done_read(banks_used, t):
            for b in banks_used:
                bk_r[b].append(t)

        pool_idx = {"sync": 0, "gp": 0, "act": 0}

        def dma(e, dst, src, deps=()):
            pool = POOL[e]
            i = pool_idx[e] % len(pool)
            pool_idx[e] += 1
            sem = pool[i]
            B.wait(e, (sem, B.cnt[sem]))
            B.waits(e, deps)
            return B.emit(e, lambda eng, d=dst, s=src: eng.dma_start(d, s),
                          inc=sem, k=16)

        def vop(fn, waits=()):
            B.waits("vec", waits)
            return B.emit("vec", fn, inc="tV")

        def vchain(fns, waits=()):
            B.waits("vec", waits)
            t = None
            for f in fns:
                if t is not None:
                    B.wait("vec", t)
                t = B.emit("vec", f, inc="tV")
            return t

        def rowv(base, slot):
            return av(base + slot * 2 * TOK, TOK, F32, 0, 1)

        def rowtmp(base, which):
            return av(base + 4 * TOK + 1024 * which, 512, F32, 0, 1)

        def strip_mms(bA, bB, src_b, src_q, first, last):
            t = B.emit("pe", lambda t_, o=ps[0:32, 512 * bA:512 * bA + 512], s=src_b,
                       st=first, sp=last: t_.matmul(o, ones_b, s, start=st, stop=sp))
            t = B.emit("pe", lambda t_, o=ps[0:32, 512 * bB:512 * bB + 512], s=src_q,
                       st=first, sp=last: t_.matmul(o, ones_b, s, start=st, stop=sp),
                       inc="tT")
            return t

        def strip_finalize_tc(tcx, tok, bA, bB, rows_off):
            rS = rowtmp(rows_off, 0)
            rQ = rowtmp(rows_off, 1)
            rT = rowtmp(rows_off, 2)
            rM = rowv(rows_off, 0)[0:1, tcx * 512:tcx * 512 + 512]
            rR = rowv(rows_off, 1)[0:1, tcx * 512:tcx * 512 + 512]
            ev = vop(lambda v: v.tensor_copy(rS, ps[0:1, 512 * bA:512 * bA + 512]),
                     waits=[tok])
            ev = vop(lambda v: v.tensor_copy(rQ, ps[0:1, 512 * bB:512 * bB + 512]),
                     waits=[ev])
            done_read([bA], ev)
            done_read([bB], ev)
            seq = [
                lambda v: v.tensor_scalar(rM, rS, 1.0 / D, None, ALU.mult),
                lambda v: v.tensor_scalar(rQ, rQ, 1.0 / D, None, ALU.mult),
                lambda v: v.tensor_tensor(rT, rM, rM, ALU.mult),
                lambda v: v.tensor_tensor(rQ, rQ, rT, ALU.subtract),
                lambda v: v.tensor_scalar(rQ, rQ, 1.0, LN_EPS, ALU.mult, ALU.add),
                lambda v: v.tensor_scalar(rR.bitcast(I32), rQ.bitcast(I32), 1, None,
                                          ALU.arith_shift_right),
                lambda v: v.tensor_tensor(rR.bitcast(I32),
                                          c_i[0:1, 0:1].broadcast_to([1, 512]),
                                          rR.bitcast(I32), ALU.subtract),
            ]
            for _ in range(4):
                seq += [
                    lambda v: v.tensor_tensor(rT, rQ, rR, ALU.mult),
                    lambda v: v.tensor_tensor(rT, rT, rR, ALU.mult),
                    lambda v: v.tensor_scalar(rT, rT, -0.5, 1.5, ALU.mult, ALU.add),
                    lambda v: v.tensor_tensor(rR, rR, rT, ALU.mult),
                ]
            return vchain(seq, waits=[ev])

        tc_ = [dma("sync", c_bf[:, :], cbf), dma("sync", c_f[:, :], cf32),
               dma("sync", c_i[:, :], ci32)]
        t_sm0 = vop(lambda v: v.memset(sm[:, :], 0.0), waits=tc_)
        for _e in ("pe", "act", "gp"):
            B.waits(_e, tc_ + [t_sm0])

        xin = [av(0, 2048, F32), av(8192, 2048, F32)]
        x0st = [av(16384, 2048, F32), av(24576, 2048, F32)]
        p0_last_ev = None
        t_xin_cons = [None, None]
        t_x0dma = [None, None]
        for tt in range(NT):
            p = tt % 2
            tl = dma("sync", xin[p][:, :], x_in[tt * 128:(tt + 1) * 128, :],
                     deps=[t_xin_cons[p]])
            bks = [0, 1, 2, 3] if p == 0 else [4, 5, 6, 7]
            B.wait("pe", tl)
            fns = []
            for dc in range(DC):
                b = bks[dc // 4]
                off = 512 * b + (dc % 4) * 128
                fns.append(lambda t_, s=xin[p][:, dc * 128:(dc + 1) * 128],
                           o=ps[:, off:off + 128], st=(dc % 4 == 0), sp=(dc % 4 == 3):
                           t_.matmul(o, s, ident_f, start=st, stop=sp))
            tmm = mm_group(bks, fns)
            t_xin_cons[p] = tmm
            ev = None
            B.waits("vec", [tmm, t_x0dma[p]])
            for j in range(4):
                ev = B.emit("vec", lambda v, s=bank(bks[j]),
                            o=x0st[p][:, j * 512:(j + 1) * 512]:
                            v.tensor_copy(o, s), inc="tV")
            done_read(bks, ev)
            t_x0dma[p] = dma("act",
                             s_x0[:, :, tt * 128:(tt + 1) * 128].transpose([1, 0, 2]),
                             x0st[p][:, :].rearrange("p (c f) -> p c f", c=16),
                             deps=[ev])
            p0_last_ev = ev

        qw = [av(65536, 8192, F32), av(81920, 8192, F32)]
        qa = av(32768, 8192, BF16)
        qacc = av(50000, 160, F32)
        t_qw_cons = [None, None]
        slast = None
        ti = 0
        tile_ranges = []
        for mi, (W, ntile, cols) in enumerate([(wup, 64, 2048), (wgate, 64, 2048),
                                               (wdown, 16, 8192)]):
            tile_ranges.append((ti, ti + ntile))
            for i in range(ntile):
                p = i % 2
                tl = dma("act", qw[p][:, 0:cols], W[i * 128:(i + 1) * 128, :],
                         deps=[t_qw_cons[p]])
                B.waits("act", [tl, slast, t_sm0])
                t = B.emit("act", lambda s_, src=qw[p][:, 0:cols], n=cols, j=ti:
                           s_.activation(qa[:, 0:n], src, AF.Abs,
                                         accum_out=qacc[:, j:j + 1]), inc="tS")
                slast = t
                t_qw_cons[p] = t
                ti += 1
        vlast = None
        for mi in range(3):
            lo, hi = tile_ranges[mi]
            vlast = vop(lambda v, a=SM_ACC + mi, lo=lo, hi=hi:
                        v.reduce_sum(sm[:, a:a + 1], qacc[:, lo:hi], axis=AX.X),
                        waits=[slast, vlast])
        ROWS1 = 30000
        p0_done = [t_xin_cons[0], t_xin_cons[1], p0_last_ev,
                   t_x0dma[0], t_x0dma[1]]
        xt1 = [av(98304 + i * 1024, 512, F32) for i in range(3)]
        xb1 = [av(101376, 512, BF16), av(101888, 512, BF16)]
        xq1 = [av(102400, 512, BF16), av(102912, 512, BF16)]
        t_x_cons = [None] * 3
        t_bq_cons = [None, None]
        pend_strip = None
        rows1_evs = []
        for tcx in range(NTC):
            for dc in range(DC):
                p3 = dc % 3
                p = dc % 2
                tl = dma("sync", xt1[p3][:, :], s_x0[dc, :, tcx * 512:(tcx + 1) * 512],
                         deps=[t_x_cons[p3]] + (p0_done if tcx == 0 and dc < 3 else []))
                tb = vop(lambda v, o=xb1[p][:, :], s=xt1[p3][:, :]:
                         v.tensor_copy(o, s), waits=[tl, t_bq_cons[p]])
                t_x_cons[p3] = tb
                B.waits("act", [tb, t_bq_cons[p]])
                tq = B.emit("act", lambda s, o=xq1[p][:, :], i_=xb1[p][:, :]:
                            s.activation(o, i_, AF.Square), inc="tS")
                if pend_strip is not None:
                    pdc, ptb, ptq, ppx = pend_strip
                    B.waits("pe", [ptb, ptq])
                    if pdc == 0:
                        B.waits("pe", bk_r[6] + bk_r[7])
                    tstr = strip_mms(6, 7, xb1[ppx][:, :], xq1[ppx][:, :],
                                     pdc == 0, pdc == DC - 1)
                    t_bq_cons[ppx] = tstr
                    if pdc == DC - 1:
                        bk_w[6] = bk_w[7] = tstr
                        bk_r[6] = []
                        bk_r[7] = []
                        rows1_evs.append(strip_finalize_tc(tcx - 1, tstr, 6, 7, ROWS1))
                pend_strip = (dc, tb, tq, p)
        pdc, ptb, ptq, ppx = pend_strip
        B.waits("pe", [ptb, ptq])
        tstr = strip_mms(6, 7, xb1[ppx][:, :], xq1[ppx][:, :], pdc == 0,
                         pdc == DC - 1)
        t_bq_cons[ppx] = tstr
        bk_w[6] = bk_w[7] = tstr
        bk_r[6] = []
        bk_r[7] = []
        rows1_evs.append(strip_finalize_tc(NTC - 1, tstr, 6, 7, ROWS1))
        rows1_tok = rows1_evs[-1]

        B.wait("pe", vlast)
        fns = [lambda t_, jj=j, a=SM_ACC + j:
               t_.matmul(ps[0:1, jj:jj + 1], sm[:, a:a + 1], ones_f[:, 0:1],
                         start=(jj == 0), stop=(jj == 2)) for j in range(3)]
        tsc = mm_group([0], fns)
        ev = vop(lambda v: v.tensor_copy(sm[0:1, SM_SC:SM_SC + 3], ps[0:1, 0:3]),
                 waits=[tsc])
        done_read([0], ev)
        B.wait("pe", ev)
        fns = [lambda t_, jj=j: t_.matmul(ps[:, 512 + jj:512 + jj + 1],
                                          ones_f[0:1, 0:128],
                                          sm[0:1, SM_SC + jj:SM_SC + jj + 1],
                                          start=(jj == 0), stop=(jj == 2))
               for j in range(3)]
        tbc = mm_group([1], fns)
        cnts = [float(E * D), float(E * D), float(D * E)]
        t = None
        for j in range(3):
            t = vchain([
                lambda v, jj=j, c=cnts[j]:
                v.tensor_scalar(sm[:, SM_TH + 2 * jj:SM_TH + 2 * jj + 1],
                                ps[:, 512 + jj:512 + jj + 1],
                                0.5 / c, 0.5 * Q_EPS, ALU.mult, ALU.add),
                lambda v, jj=j:
                v.tensor_scalar(sm[:, SM_TH + 2 * jj + 1:SM_TH + 2 * jj + 2],
                                sm[:, SM_TH + 2 * jj:SM_TH + 2 * jj + 1],
                                -1.0, None, ALU.mult),
            ], waits=[tbc, t])
        done_read([1], t)
        t_scales = vchain([
            lambda v: v.tensor_tensor(sm[:, SM_RS:SM_RS + 1],
                                      sm[:, SM_TH + 4:SM_TH + 5],
                                      sm[:, SM_TH + 2:SM_TH + 3], ALU.mult),
            lambda v: v.tensor_scalar(sm[:, SM_RS:SM_RS + 1],
                                      sm[:, SM_RS:SM_RS + 1], 0.5, None, ALU.mult),
        ], waits=[t])

        def ln_normalize(src_x, rows_off, rows_tok, dst_dram, dst_off, extra_deps,
                         scr=0):
            LNT = dst_off + DC * TOK if dst_dram is None else scr
            xt = [av(LNT + 1024 * i, 512, F32) for i in range(3)]
            tmp = av(LNT + 3072, 512, F32)
            xo = [av(LNT + 4096, 512, BF16), av(LNT + 4608, 512, BF16)]
            rM = rowv(rows_off, 0)
            rR = rowv(rows_off, 1)
            B.wait("pe", rows_tok)
            t_x_cons = [None, None, None]
            t_o_cons = [None, None]
            out_toks = []
            tlast = None
            for tc in range(NTC):
                t0 = mm_group([0], [lambda t_, s=rM[0:1, tc * 512:tc * 512 + 512]:
                                    t_.matmul(bank(0), ones_f[0:1, 0:128], s,
                                              start=True, stop=True)])
                t1 = mm_group([1], [lambda t_, s=rR[0:1, tc * 512:tc * 512 + 512]:
                                    t_.matmul(bank(1), ones_f[0:1, 0:128], s,
                                              start=True, stop=True)])
                tv = None
                for dc in range(DC):
                    p = dc % 3
                    po = dc % 2
                    tl = dma("sync", xt[p][:, :], src_x[dc, :, tc * 512:(tc + 1) * 512],
                             deps=[t_x_cons[p], rows_tok]
                             + (extra_deps if tc == 0 and dc < 3 else []))
                    dst = (xo[po][:, :] if dst_dram is not None
                           else av(dst_off + dc * TOK + tc * 512, 512, BF16))
                    tv = vchain([
                        lambda v, s=xt[p][:, :]:
                        v.tensor_tensor(tmp, s, bank(0), ALU.subtract),
                        lambda v, o=dst: v.tensor_tensor(o, tmp, bank(1), ALU.mult),
                    ], waits=[tl, t0, t1, t_o_cons[po], tlast])
                    t_x_cons[p] = tv
                    tlast = tv
                    if dst_dram is not None:
                        t_o_cons[po] = dma("act",
                                           dst_dram[tc][:, dc * 512:(dc + 1) * 512],
                                           xo[po][:, :], deps=[tv])
                        out_toks.append(t_o_cons[po])
                done_read([0], tv)
                done_read([1], tv)
            if dst_dram is not None:
                return out_toks
            return [tlast]

        xln1_done = ln_normalize(s_x0, ROWS1, rows1_tok, s_xln, 0, [], scr=98304)
        _go("ln1")

        B.waits("act", p0_done)
        s1b = [av(0, 8192, BF16), av(8192, 8192, BF16)]
        s2b = [av(16384, 8192, BF16), av(24576, 8192, BF16)]
        t2b = [av(32768, 8192, BF16), av(40960, 8192, BF16)]
        tst = [av(49152, 8192, BF16), av(57344, 8192, BF16)]
        t_sgn_gp = [None, None]
        t_t2_cons = [None, None]
        t_tst_cons = [[], []]
        B.waits("act", [t_scales])
        for W, SOFF, ntile, cols, thj in ((wup, 0, 64, 2048, 0),
                                          (wgate, 2048, 64, 2048, 1),
                                          (wdown, None, 16, 8192, 2)):
            nth_c = SM_TH + 2 * thj + 1
            th_c = SM_TH + 2 * thj
            for i in range(ntile):
                p = i % 2
                tl = dma("act", qw[p][:, 0:cols], W[i * 128:(i + 1) * 128, :],
                         deps=[t_qw_cons[p]])
                B.waits("act", [tl, t_sgn_gp[p]])
                ts1 = B.emit("act", lambda s, o=s1b[p], src=qw[p][:, 0:cols],
                             c=nth_c, n=cols:
                             s.activation(o[:, 0:n], src, AF.Sign,
                                          bias=sm[:, c:c + 1]), inc="tS")
                ts2 = B.emit("act", lambda s, o=s2b[p], src=qw[p][:, 0:cols],
                             c=th_c, n=cols:
                             s.activation(o[:, 0:n], src, AF.Sign,
                                          bias=sm[:, c:c + 1]), inc="tS")
                t_qw_cons[p] = ts2
                tgp = vop(lambda v, n=cols, pp=p:
                          v.tensor_add(t2b[pp][:, 0:n], s1b[pp][:, 0:n],
                                       s2b[pp][:, 0:n]),
                          waits=[ts1, ts2, t_t2_cons[p]])
                t_sgn_gp[p] = tgp
                B.wait("pe", tgp)
                ngrp = cols // 512
                evtok = None
                tmm = None
                for g in range(ngrp):
                    b = 2 + (g % 2)
                    fns = []
                    for cc in range(4):
                        c = g * 4 + cc
                        fns.append(lambda t_, src=t2b[p][:, c * 128:(c + 1) * 128],
                                   o=ps[:, 512 * b + cc * 128:512 * b + (cc + 1) * 128],
                                   st=(cc == 0), sp=(cc == 3):
                                   t_.matmul(o, src, ident_b, start=st, stop=sp))
                    tmm = mm_group([b], fns)
                    evtok = vop(lambda v, s=bank(b), o=tst[p][:, g * 512:(g + 1) * 512]:
                                v.tensor_copy(o, s), waits=[tmm] + (t_tst_cons[p] or []))
                    done_read([b], evtok)
                t_t2_cons[p] = tmm
                if cols == 2048:
                    t_tst_cons[p] = [dma("sync", s_tug[i][:, SOFF:SOFF + 2048],
                                         tst[p][:, 0:2048], deps=[evtok])]
                else:
                    t_tst_cons[p] = [dma("sync", s_td[i], tst[p][:, 0:8192],
                                         deps=[evtok])]
        B.barrier()

        def ffn(xln_ready, src_x, dst_x, rows_off):
            hid = av(0, 32768, BF16)
            xlnb = av(32768, 8192, BF16)
            stu = [av(40960 + 4096 * i, 4096, BF16) for i in range(3)]
            tdb = [av(53248 + 8192 * i, 8192, BF16) for i in range(3)]
            x0r = [av(77824, 512, F32), av(78848, 512, F32)]
            x1s = [av(79872, 512, F32), av(80896, 512, F32)]
            silu = [av(81920, 512, BF16), av(82432, 512, BF16)]
            x1b = [av(82944, 512, BF16), av(83456, 512, BF16)]
            x1q = [av(83968, 512, BF16), av(84480, 512, BF16)]
            t_xlnb_cons = None
            t_stu_cons = [None] * 3
            t_td_cons = [None] * 3
            t_x0_cons = [None, None]
            t_x1_cons = [None, None]
            t_sil_cons = [None, None]
            t_bq_cons = [None, None]
            pend_strip = None
            hid_tok = {}
            rows_evs = []
            outtoks = []
            want_stats = rows_off is not None
            for blk in range(NBLK):
                t_xlnb = dma("sync", xlnb[:, :], s_xln[blk],
                             deps=[t_xlnb_cons] + (xln_ready if blk == 0 else []))
                for e in range(EC):
                    pe_ = e % 3
                    tsu = dma("sync", stu[pe_][:, :], s_tug[e],
                              deps=[t_stu_cons[pe_]])
                    bu, bg = (0, 1) if e % 2 == 0 else (2, 3)
                    B.waits("pe", [tsu, t_xlnb])
                    fns = [lambda t_, s=stu[pe_][:, d * 128:(d + 1) * 128],
                           m=xlnb[:, d * 512:(d + 1) * 512], o=bank(bu),
                           st=(d == 0), sp=(d == DC - 1):
                           t_.matmul(o, s, m, start=st, stop=sp) for d in range(DC)]
                    tu_mm = mm_group([bu], fns)
                    t_stu_cons[pe_] = tu_mm
                    fns = [lambda t_, s=stu[pe_][:, 2048 + d * 128:2048 + (d + 1) * 128],
                           m=xlnb[:, d * 512:(d + 1) * 512], o=bank(bg),
                           st=(d == 0), sp=(d == DC - 1):
                           t_.matmul(o, s, m, start=st, stop=sp) for d in range(DC)]
                    tg_mm = mm_group([bg], fns)
                    t_stu_cons[pe_] = tg_mm
                    if e == EC - 1:
                        t_xlnb_cons = tg_mm
                    sl_ = e % 2
                    B.waits("act", [tu_mm, t_sil_cons[sl_]])
                    tsl = B.emit("act", lambda s, o=silu[sl_][:, :], i_=bank(bu):
                                 s.activation(o, i_, AF.Silu,
                                              scale=sm[:, SM_TH:SM_TH + 1]), inc="tS")
                    done_read([bu], tsl)
                    thd = vop(lambda v, o=hid[:, e * 512:(e + 1) * 512],
                              a=silu[sl_][:, :], b=bank(bg):
                              v.tensor_tensor(o, a, b, ALU.mult),
                              waits=[tsl, tg_mm])
                    done_read([bg], thd)
                    t_sil_cons[sl_] = thd
                    hid_tok[e] = thd
                tstr = None
                for dc in range(DC):
                    pd = dc % 3
                    px = dc % 2
                    ttd = dma("sync", tdb[pd][:, :], s_td[dc], deps=[t_td_cons[pd]])
                    tx0 = dma("act", x0r[px][:, :],
                              src_x[dc, :, blk * 512:(blk + 1) * 512],
                              deps=[t_x0_cons[px]])
                    bd = 4 + px
                    B.wait("pe", ttd)
                    if dc == 0:
                        B.waits("pe", [hid_tok[e] for e in range(EC)])
                    fns = [lambda t_, s=tdb[pd][:, e * 128:(e + 1) * 128],
                           m=hid[:, e * 512:(e + 1) * 512], o=bank(bd),
                           st=(e == 0), sp=(e == EC - 1):
                           t_.matmul(o, s, m, start=st, stop=sp) for e in range(EC)]
                    td_mm = mm_group([bd], fns)
                    t_td_cons[pd] = td_mm
                    tev = vop(lambda v, o=x1s[px][:, :], i_=bank(bd), x_=x0r[px][:, :]:
                              v.scalar_tensor_tensor(o, i_, sm[:, SM_RS:SM_RS + 1],
                                                     x_, ALU.mult, ALU.add),
                              waits=[td_mm, tx0, t_x1_cons[px]])
                    done_read([bd], tev)
                    t_x0_cons[px] = tev
                    tod = dma("act", dst_x[dc, :, blk * 512:(blk + 1) * 512],
                              x1s[px][:, :], deps=[tev])
                    outtoks.append(tod)
                    t_x1_cons[px] = tod
                    if want_stats:
                        tb = vop(lambda v, o=x1b[px][:, :], s=x1s[px][:, :]:
                                 v.tensor_copy(o, s), waits=[tev, t_bq_cons[px]])
                        B.waits("act", [tb, t_bq_cons[px]])
                        tq = B.emit("act", lambda s, o=x1q[px][:, :], i_=x1b[px][:, :]:
                                    s.activation(o, i_, AF.Square), inc="tS")
                        if pend_strip is not None:
                            pdc, ptb, ptq, ppx = pend_strip
                            B.waits("pe", [ptb, ptq])
                            if pdc == 0:
                                B.waits("pe", bk_r[6] + bk_r[7])
                            tstr = strip_mms(6, 7, x1b[ppx][:, :], x1q[ppx][:, :],
                                             pdc == 0, pdc == DC - 1)
                            t_bq_cons[ppx] = tstr
                        pend_strip = (dc, tb, tq, px)
                if want_stats:
                    pdc, ptb, ptq, ppx = pend_strip
                    B.waits("pe", [ptb, ptq])
                    tstr = strip_mms(6, 7, x1b[ppx][:, :], x1q[ppx][:, :],
                                     pdc == 0, pdc == DC - 1)
                    t_bq_cons[ppx] = tstr
                    pend_strip = None
                    bk_w[6] = bk_w[7] = tstr
                    bk_r[6] = []
                    bk_r[7] = []
                    rows_evs.append(strip_finalize_tc(blk, tstr, 6, 7, rows_off))
            rt = rows_evs[-1] if want_stats else None
            return rt, outtoks

        ROWSF = 85504
        if _go("ffn1"):
            rows2_tok, x1toks = ffn(xln1_done, s_x0, s_x1, ROWSF)

        XLN2 = 0
        if _go("ln2"):
            B.barrier()
            xln2_done = ln_normalize(s_x1, ROWSF, rows2_tok, None, XLN2, x1toks)

        _do_attn = _go("attn_a")
        if not _do_attn:
            xln2_done = []
        B.barrier()
        A0 = DC * TOK + 4096 + 64
        qb = av(A0, TOK, BF16)
        kb = av(A0 + TOK, TOK, BF16)
        vb = av(A0 + 2 * TOK, TOK, BF16)
        cvb = av(A0 + 4 * TOK + 6144, TOK, BF16)
        e0p = [av(A0 + 3 * TOK, TOK, BF16), cvb]
        kb2 = av(A0 + 5 * TOK + 6272 + 4096, TOK, BF16)
        kbufs = [kb, kb2]
        wqh = [av(A0 + 4 * TOK + 2048 * i, 2048, BF16) for i in range(3)]
        wrep = av(A0 + 5 * TOK + 6144, 96, BF16)
        kmb = av(A0 + 5 * TOK + 6240, 2, BF16)
        CSTG = A0 + 5 * TOK + 6272
        cstg = av(CSTG, 2048, F32)

        B.waits("pe", xln2_done)
        t_wqh_cons = [None] * 3
        t_cv_dma = None
        cv_toks = []
        t_e0_cons = [None, None]
        t_cstg_cons = []
        nr0 = 32 * min(4, NKC)
        nr1 = 32 * max(0, NKC - 4)
        wrepp = [wrep[:, 32 * i:32 * i + 32] for i in range(3)]

        def load_wqh(h):
            toks = []
            for ci in range(3):
                src = wqkv[:, ci * D + h * 128:ci * D + (h + 1) * 128]
                toks.append(dma("gp", wqh[ci][:, :].rearrange("p (c f) -> p c f",
                                                              c=16),
                                src.rearrange("(c p) f -> p c f", p=128),
                                deps=[t_wqh_cons[ci]]))
            return toks

        def qkv_units(h, twq, cis):
            evs = {}
            for ci in cis:
                for th in range(NUQ):
                    nb = UW // 512
                    b0 = 2 * (th % 2) if ci == 2 else 0
                    bks = list(range(b0, b0 + nb))
                    B.wait("pe", twq[ci])
                    fns = []
                    for d in range(DC):
                        for nn in range(nb):
                            fns.append(lambda t_,
                                       s=wqh[ci][:, d * 128:(d + 1) * 128],
                                       m=av(XLN2 + d * TOK + th * UW + nn * 512, 512),
                                       o=bank(b0 + nn), st=(d == 0),
                                       sp=(d == DC - 1):
                                       t_.matmul(o, s, m, start=st, stop=sp))
                    tmm = mm_group(bks, fns)
                    if th == NUQ - 1:
                        t_wqh_cons[ci] = tmm
                    dstb = (qb, kb, vb)[ci]
                    if ci == 2:
                        ev = vop(lambda v, o=dstb[:, th * UW:(th + 1) * UW],
                                 s=ps[:, 512 * b0:512 * b0 + UW]:
                                 v.tensor_copy(o, s), waits=[tmm, t_cv_dma])
                    else:
                        B.waits("act", [tmm])
                        ev = B.emit("act", lambda s_, o=dstb[:, th * UW:(th + 1) * UW],
                                    s=ps[:, 0:UW]:
                                    s_.copy(o, s), inc="tS")
                    done_read(bks, ev)
                evs[ci] = ev
            return evs

        def sq_strips(buf):
            fns = []
            for kc in range(NKC):
                bb = kc // 4
                jj = kc % 4
                fns.append(lambda t_, s=buf[:, kc * 512:(kc + 1) * 512],
                           o=ps[32 * jj:32 * jj + 32, 512 * bb:512 * bb + 512],
                           tp=(0, 32 * jj):
                           t_.matmul(o, ones_b, s, start=True, stop=True,
                                     tile_position=tp))
            return mm_group([0, 1] if nr1 else [0], fns)

        def strip_max(dst_col, tok):
            seq = [lambda v: v.reduce_max(sm[0:nr0, dst_col:dst_col + 1],
                                          ps[0:nr0, 0:512], axis=AX.X)]
            if nr1:
                seq += [
                    lambda v: v.reduce_max(sm[0:nr1, SM_TMP:SM_TMP + 1],
                                           ps[0:nr1, 512:1024], axis=AX.X),
                    lambda v: v.tensor_tensor(sm[:, dst_col:dst_col + 1],
                                              sm[:, dst_col:dst_col + 1],
                                              sm[:, SM_TMP:SM_TMP + 1], ALU.max),
                ]
            return vchain(seq, waits=[tok])

        def norms(evq, evk, kbv, wait_exp=None):
            tqs = vop(lambda v: v.tensor_tensor(e0p[0][:, 0:TOK], qb[:, 0:TOK],
                                                qb[:, 0:TOK], ALU.mult),
                      waits=[evq, t_e0_cons[0]])
            B.wait("pe", tqs)
            tq_str = sq_strips(e0p[0])
            tqm = strip_max(SM_QM, tq_str)
            done_read([0], tqm)
            done_read([1], tqm)
            tks = vop(lambda v: v.tensor_tensor(e0p[0][:, 0:TOK], kbv[:, 0:TOK],
                                                kbv[:, 0:TOK], ALU.mult),
                      waits=[evk, tq_str])
            B.wait("pe", tks)
            tk_str = sq_strips(e0p[0])
            t_e0_cons[0] = tk_str
            tkm = strip_max(SM_KM, tk_str)
            done_read([0], tkm)
            done_read([1], tkm)
            t = vop(lambda v: v.tensor_copy(kmb[:, 0:2], sm[:, SM_QM:SM_QM + 2]),
                    waits=[tqm, tkm])
            B.wait("pe", t)
            tt_ = mm_group([2], [
                lambda t_: t_.matmul(ps[0:1, 1024:1152], kmb[:, 0:1], ident_b,
                                     start=True, stop=False),
                lambda t_: t_.matmul(ps[0:1, 1152:1280], kmb[:, 1:2], ident_b,
                                     start=False, stop=True),
            ])
            t = vchain([
                lambda v: v.reduce_max(sm[0:1, SM_QM:SM_QM + 1], ps[0:1, 1024:1152],
                                       axis=AX.X),
                lambda v: v.reduce_max(sm[0:1, SM_KM:SM_KM + 1], ps[0:1, 1152:1280],
                                       axis=AX.X),
                lambda v: v.tensor_tensor(sm[0:1, SM_BIAS:SM_BIAS + 1],
                                          sm[0:1, SM_QM:SM_QM + 1],
                                          sm[0:1, SM_KM:SM_KM + 1], ALU.add),
                lambda v: v.tensor_scalar(sm[0:1, SM_BIAS:SM_BIAS + 1],
                                          sm[0:1, SM_BIAS:SM_BIAS + 1],
                                          -0.505 * ISQ, None, ALU.mult),
            ], waits=[tt_])
            done_read([2], t)
            B.waits("pe", [t] + bk_r[2])
            tt_ = mm_group([2], [lambda t_: t_.matmul(ps[:, 1024:1025],
                                                      ones_f[0:1, 0:128],
                                                      sm[0:1, SM_BIAS:SM_BIAS + 1],
                                                      start=True, stop=True)])
            tbias = vop(lambda v: v.tensor_copy(sm[:, SM_BIAS:SM_BIAS + 1],
                                                ps[:, 1024:1025]),
                        waits=[tt_, wait_exp])
            done_read([2], tbias)
            return tbias

        CB = [6, 6, 6, 6, 7, 7, 7, 7]
        CBANKS = sorted(set(CB[:NKC]))

        def cpos(kc):
            return CB[kc], 32 * (kc % 4)

        def emit_cmms(qt, pq, e0b, t):
            B.wait("pe", t)
            tcm = None
            for kc in range(NKC):
                bb, jj = cpos(kc)
                if qt == 0 and kc == 0:
                    for b_ in CBANKS:
                        B.waits("pe", bk_r[b_])
                tcm = B.emit("pe", lambda t_, m=e0b[:, kc * 512:(kc + 1) * 512],
                             w=wrepp[pq],
                             o=ps[jj:jj + 32, 512 * bb:512 * bb + 512],
                             st=(qt == 0), sp=(qt == NT - 1), tp=(0, jj):
                             t_.matmul(o, w, m, start=st, stop=sp,
                                       tile_position=tp),
                             inc="tT" if kc == NKC - 1 else None)
            return tcm

        def scores_loop(tbias, kbv, fillers, sc_toks):
            pend = []
            tcm = None
            pre = None

            def sc_group(qt, kq):
                sc_bks = (2, 3) if (qt + kq) % 2 == 0 else (4, 5)
                nn2 = KQW // 512
                fns = [lambda t_, s=qb[:, qt * 128:(qt + 1) * 128],
                       m=kbv[:, kq * 1024 + nn * 512:kq * 1024 + (nn + 1) * 512],
                       o=bank(sc_bks[nn]):
                       t_.matmul(o, s, m, start=True, stop=True)
                       for nn in range(nn2)]
                return mm_group(list(sc_bks[:nn2]), fns), sc_bks, nn2

            for qt in range(NT):
                pq = qt % 2
                e0b = e0p[pq]
                te = None
                tvz = None
                for kq in range(NKQ):
                    if kq == 0 and pre is not None:
                        tmm, sc_bks, nn2 = pre
                    else:
                        tmm, sc_bks, nn2 = sc_group(qt, kq)
                    sc_toks[qt] = tmm
                    B.waits("act", [tmm, tbias, t_e0_cons[pq]])
                    if kq < NKQ // 2:
                        te = B.emit("act",
                                    lambda s_, o=e0b[:, kq * 1024:kq * 1024 + KQW],
                                    i_=ps[:, 512 * sc_bks[0]:512 * sc_bks[0] + KQW]:
                                    s_.activation(o, i_, AF.Exp, scale=ISQ,
                                                  bias=sm[:, SM_BIAS:SM_BIAS + 1]),
                                    inc="tS")
                        tvz = vop(lambda v, s=e0b[:, kq * 1024:kq * 1024 + KQW],
                                  z=sm[:, SM_Z + 8 * pq + kq:SM_Z + 8 * pq + kq + 1]:
                                  v.reduce_sum(z, s, axis=AX.X),
                                  waits=[te, tvz])
                    else:
                        te = B.emit("act", lambda s_,
                                    o=e0b[:, kq * 1024:kq * 1024 + KQW],
                                    i_=ps[:, 512 * sc_bks[0]:512 * sc_bks[0] + KQW],
                                    z=sm[:, SM_Z + 8 * pq + kq:SM_Z + 8 * pq + kq + 1]:
                                    s_.activation(o, i_, AF.Exp, scale=ISQ,
                                                  bias=sm[:, SM_BIAS:SM_BIAS + 1],
                                                  accum_out=z), inc="tS")
                    done_read(list(sc_bks[:nn2]), te)
                t = vchain([
                    lambda v, n=NKQ, o=8 * pq, pq=pq: v.reduce_sum(
                        sm[:, SM_ZS + pq:SM_ZS + pq + 1],
                        sm[:, SM_Z + o:SM_Z + o + n], axis=AX.X),
                    lambda v, pq=pq: v.reciprocal(sm[:, SM_IZ + pq:SM_IZ + pq + 1],
                                                  sm[:, SM_ZS + pq:SM_ZS + pq + 1]),
                    lambda v, w=wrepp[pq], pq=pq: v.tensor_copy(
                        w, sm[:, SM_IZ + pq:SM_IZ + pq + 1].broadcast_to([128, 32])),
                ], waits=[te, tvz, t_e0_cons[pq]])
                pend.append((qt, pq, e0b, t))
                if len(pend) > 1:
                    pqt, ppq, pe0b, pt = pend.pop(0)
                    tcm = emit_cmms(pqt, ppq, pe0b, pt)
                    t_e0_cons[ppq] = tcm
                for f in fillers.get(qt, []):
                    f()
                pre = sc_group(qt + 1, 0) if qt + 1 < NT else None
            for pqt, ppq, pe0b, pt in pend:
                tcm = emit_cmms(pqt, ppq, pe0b, pt)
                t_e0_cons[ppq] = tcm
            return tcm, te

        def cbcast_part1(cstrip_tok):
            ev = None
            for gi, b_ in enumerate(CBANKS):
                ev = vop(lambda v, o=cstg[:, gi * 512:(gi + 1) * 512], s=bank(b_):
                         v.tensor_copy(o, s),
                         waits=[cstrip_tok, ev] + ([] if gi else (t_cstg_cons or [])))
                done_read([b_], ev)
            fix = []
            for kc in range(NKC):
                if 32 * (kc % 4) == 96:
                    gi = CBANKS.index(CB[kc])
                    fix.append(dma("gp",
                                   cstg[0:1, (2 + kc // 4) * 512:
                                             (3 + kc // 4) * 512],
                                   cstg[96:97, gi * 512:(gi + 1) * 512],
                                   deps=[ev]))
            return ev, fix

        def cbcast_part2(h, ev, fix, evv):
            t_cvw = None
            cstg_rd = []
            for kc in range(NKC):
                bb = 2 + kc % 2
                _, jj = cpos(kc)
                gi = CBANKS.index(CB[kc])
                if jj == 96:
                    r0, c0 = 0, (2 + kc // 4) * 512
                    B.waits("pe", [ev, fix[kc // 4]])
                else:
                    r0, c0 = jj, gi * 512
                    B.wait("pe", ev)
                tt_ = mm_group([bb], [
                    lambda t_, o=bank(bb), w=ones_f[r0:r0 + 1, 0:128],
                    s=cstg[r0:r0 + 1, c0:c0 + 512]:
                    t_.matmul(o, w, s, start=True, stop=True)])
                cstg_rd.append(tt_)
                t_cvw = vop(lambda v, o=cvb[:, kc * 512:(kc + 1) * 512],
                            a=vb[:, kc * 512:(kc + 1) * 512], b=bank(bb):
                            v.tensor_tensor(o, a, b, ALU.mult),
                            waits=[tt_, evv, t_cv_dma] +
                                  ([t_e0_cons[1]] if kc == 0 else []))
                done_read([bb], t_cvw)
            td = dma("act", s_cv[:, :, h * 512:(h + 1) * 512].transpose([1, 0, 2]),
                     cvb[:, 0:TOK].rearrange("p (c f) -> p c f", c=NTC),
                     deps=[t_cvw])
            return td, cstg_rd

        def mk_fillers(hn, sc_toks, twq):
            fillers = {}
            kdst = kbufs[hn % 2]
            ev_last = {}
            st = {}
            units = []
            for th in range(NUQ):
                units.append((1, th))
                units.append((0, th))
            for u, (ci, th) in enumerate(units):
                for s4 in range(4):
                    slot = 4 * u + s4

                    def sub(ci=ci, th=th, s4=s4):
                        if s4 == 0:
                            B.waits("pe", bk_r[0])
                            B.waits("pe", bk_r[1])
                            B.wait("pe", twq[ci])
                        last = None
                        for dd in range(4):
                            d = s4 * 4 + dd
                            for nn in range(2):
                                fin = (s4 == 3 and dd == 3 and nn == 1)
                                last = B.emit(
                                    "pe", lambda t_,
                                    s=wqh[ci][:, d * 128:(d + 1) * 128],
                                    m=av(XLN2 + d * TOK + th * UW + nn * 512,
                                         512),
                                    o=bank(nn), st_=(d == 0), sp=(d == DC - 1):
                                    t_.matmul(o, s, m, start=st_, stop=sp),
                                    inc="tT" if fin else None)
                        if s4 == 3:
                            st[(ci, th)] = last
                            for b in (0, 1):
                                bk_w[b] = last
                                bk_r[b] = []
                    fillers.setdefault(slot, []).append(sub)

                def evac(ci=ci, th=th):
                    tmm = st[(ci, th)]
                    if th == NUQ - 1:
                        t_wqh_cons[ci] = tmm
                    dstb = qb if ci == 0 else kdst
                    w = [tmm]
                    if ci == 0:
                        w.append(sc_toks[8 * th + 7])
                    ev = vop(lambda v, o=dstb[:, th * UW:(th + 1) * UW],
                             s=ps[:, 0:UW]: v.tensor_copy(o, s), waits=w)
                    done_read([0, 1], ev)
                    ev_last[ci] = ev
                fillers.setdefault(4 * u + 3, []).append(evac)
            return fillers, ev_last

        twq = load_wqh(0)
        if _do_attn:
            evs = qkv_units(0, twq, (0, 1, 2))
            evq, evk, evv = evs[0], evs[1], evs[2]
            tbias = norms(evq, evk, kb)
        last_te = None
        for h in range(H if _do_attn else 0):
            sc_toks = {}
            if h + 1 < H:
                twq = load_wqh(h + 1)
                fillers, ev_last = mk_fillers(h + 1, sc_toks, twq)
            else:
                fillers = {}
            cstrip_tok, last_te = scores_loop(tbias, kbufs[h % 2], fillers,
                                              sc_toks)
            for b_ in CBANKS:
                bk_w[b_] = cstrip_tok
                bk_r[b_] = []
            myevv = evv
            ev1, fix1 = cbcast_part1(cstrip_tok)
            if h + 1 < H:
                tbias = norms(ev_last[0], ev_last[1], kbufs[(h + 1) % 2],
                              last_te)
            t_cv_dma, t_cstg_cons = cbcast_part2(h, ev1, fix1, myevv)
            t_e0_cons[1] = t_cv_dma
            cv_toks.append(t_cv_dma)
            if h + 1 < H:
                evs = qkv_units(h + 1, twq, (2,))
                evv = evs[2]

        _do_attnb = _go("attn_b")
        B.barrier()
        ROWS3 = 60000
        wob = av(0, 32768, BF16)
        cvt = [av(32768, 8192, BF16), av(40960, 8192, BF16)]
        x1r = [av(49152 + 1024 * i, 512, F32) for i in range(4)]
        x2b = [av(53248, 512, BF16), av(53760, 512, BF16)]
        x2q = [av(54272, 512, BF16), av(54784, 512, BF16)]
        X2K = 80000
        x2k = [av(X2K + 1024 * dc, 512, F32) for dc in range(DC)]
        lntmp = av(X2K + 16384, 512, F32)
        lnout = [av(X2K + 17408, 512, BF16), av(X2K + 17920, 512, BF16)]
        two_toks = []
        for fc in range(DC if _do_attnb else 0):
            two_toks.append(dma("gp", wob[:, fc * 2048:(fc + 1) * 2048],
                                wo[fc * 128:(fc + 1) * 128, :], deps=[]))
        t_cvt_cons = [None, None]
        t_x1_cons = [None] * 4
        t_x2_cons = {}
        t_bq_cons = [None, None]
        t_lnout_cons = [None, None]
        rM3 = rowv(ROWS3, 0)
        rR3 = rowv(ROWS3, 1)
        pend_strip = None
        pend_ln = None
        rows3_evs = []
        x2toks = []
        xln3_toks = []

        ln_prev = [None]

        def ln3_dc(tcx, dc, t0, t1):
            p = dc % 2
            tv = vchain([
                lambda v, s=x2k[dc][:, :]:
                v.tensor_tensor(lntmp, s, ps[:, 2048:2560], ALU.subtract),
                lambda v, o=lnout[p][:, :]:
                v.tensor_tensor(o, lntmp, ps[:, 2560:3072], ALU.mult),
            ], waits=[t0, t1, t_lnout_cons[p], ln_prev[0]])
            ln_prev[0] = tv
            tok = dma("gp", s_xln[tcx][:, dc * 512:(dc + 1) * 512],
                      lnout[p][:, :], deps=[tv])
            t_lnout_cons[p] = tok
            xln3_toks.append(tok)
            return tv
        for tcx in range(NTC if _do_attnb else 0):
            pt = tcx % 2
            tcv = dma("sync", cvt[pt][:, :], s_cv[tcx],
                      deps=[t_cvt_cons[pt]] + (cv_toks if tcx < 2 else []))
            tstr = None
            for dc in range(DC):
                pd = dc % 2
                p4 = dc % 4
                bo = pd
                if pend_ln is not None:
                    tvln = ln3_dc(pend_ln[0], dc, pend_ln[1], pend_ln[2])
                    if dc == DC - 1:
                        done_read([4], tvln)
                        done_read([5], tvln)
                        pend_ln = None
                B.waits("pe", two_toks + [tcv])
                fns = [lambda t_,
                       s=wob[:, hh * 2048 + dc * 128:hh * 2048 + (dc + 1) * 128],
                       m=cvt[pt][:, hh * 512:(hh + 1) * 512], o=bank(bo),
                       st=(hh == 0), sp=(hh == H - 1):
                       t_.matmul(o, s, m, start=st, stop=sp) for hh in range(H)]
                tmm = mm_group([bo], fns)
                if dc >= DC - 2:
                    t_cvt_cons[pt] = tmm
                tx1 = dma("sync", x1r[p4][:, :],
                          s_x1[dc, :, tcx * 512:(tcx + 1) * 512],
                          deps=[t_x1_cons[p4]])
                tev = vop(lambda v, o=x2k[dc][:, :], i_=bank(bo),
                          x_=x1r[p4][:, :]:
                          v.tensor_tensor(o, i_, x_, ALU.add),
                          waits=[tmm, tx1, t_x2_cons.get(dc)])
                done_read([bo], tev)
                t_x1_cons[p4] = tev
                tod = dma("act", s_x2[dc, :, tcx * 512:(tcx + 1) * 512],
                          x2k[dc][:, :], deps=[tev])
                x2toks.append(tod)
                t_x2_cons[dc] = tod
                tb = vop(lambda v, o=x2b[pd][:, :], s=x2k[dc][:, :]:
                         v.tensor_copy(o, s), waits=[tev, t_bq_cons[pd]])
                B.waits("act", [tb, t_bq_cons[pd]])
                tq = B.emit("act", lambda s, o=x2q[pd][:, :], i_=x2b[pd][:, :]:
                            s.activation(o, i_, AF.Square), inc="tS")
                if pend_strip is not None:
                    pdc, ptb, ptq, ppx = pend_strip
                    B.waits("pe", [ptb, ptq])
                    if pdc == 0:
                        B.waits("pe", bk_r[2] + bk_r[3])
                    tstr = strip_mms(2, 3, x2b[ppx][:, :], x2q[ppx][:, :],
                                     pdc == 0, pdc == DC - 1)
                    t_bq_cons[ppx] = tstr
                pend_strip = (dc, tb, tq, pd)
            pdc, ptb, ptq, ppx = pend_strip
            B.waits("pe", [ptb, ptq])
            tstr = strip_mms(2, 3, x2b[ppx][:, :], x2q[ppx][:, :],
                             pdc == 0, pdc == DC - 1)
            t_bq_cons[ppx] = tstr
            pend_strip = None
            bk_w[2] = bk_w[3] = tstr
            bk_r[2] = []
            bk_r[3] = []
            rows3_evs.append(strip_finalize_tc(tcx, tstr, 2, 3, ROWS3))
            B.waits("pe", [rows3_evs[-1]] + bk_r[4] + bk_r[5])
            t0 = mm_group([4], [lambda t_, s=rM3[0:1, tcx * 512:tcx * 512 + 512]:
                                t_.matmul(ps[:, 2048:2560], ones_f[0:1, 0:128],
                                          s, start=True, stop=True)])
            t1 = mm_group([5], [lambda t_, s=rR3[0:1, tcx * 512:tcx * 512 + 512]:
                                t_.matmul(ps[:, 2560:3072], ones_f[0:1, 0:128],
                                          s, start=True, stop=True)])
            pend_ln = (tcx, t0, t1)
        if _do_attnb:
            rows3_tok = rows3_evs[-1]
            ltcx, t0, t1 = pend_ln
            tvln = None
            for dc in range(DC):
                tvln = ln3_dc(ltcx, dc, t0, t1)
            done_read([4], tvln)
            done_read([5], tvln)
            pend_ln = None

        if _go("ln3"):
            xln3_done = xln3_toks
        if _go("ffn2"):
            B.barrier()
            _, x3toks = ffn(xln3_done, s_x2, s_x3, None)

        _do_final = _go("final")
        B.barrier()
        x3r = [av(4096 * i, 2048, F32) for i in range(3)]
        ost = [av(16384 + 4096 * i, 2048, F32) for i in range(3)]
        t_in_cons = [None, None, None]
        t_ost = [None, None, None]
        for tt in range(NT if _do_final else 0):
            p = tt % 3
            tl = dma("sync", x3r[p][:, :].rearrange("p (c f) -> p c f", c=16),
                     s_x3[:, :, tt * 128:(tt + 1) * 128].transpose([1, 0, 2]),
                     deps=[t_in_cons[p]] + (x3toks if tt < 3 else []))
            bks = [0, 1, 2, 3] if tt % 2 == 0 else [4, 5, 6, 7]
            B.wait("pe", tl)
            fns = []
            for dc in range(DC):
                b = bks[dc // 4]
                off = 512 * b + (dc % 4) * 128
                fns.append(lambda t_, s=x3r[p][:, dc * 128:(dc + 1) * 128],
                           o=ps[:, off:off + 128], st=(dc % 4 == 0),
                           sp=(dc % 4 == 3):
                           t_.matmul(o, s, ident_f, start=st, stop=sp))
            tmm = mm_group(bks, fns)
            t_in_cons[p] = tmm
            ev = None
            B.waits("vec", [tmm, t_ost[p]])
            for j in range(4):
                ev = B.emit("vec", lambda v, s=bank(bks[j]),
                            o=ost[p][:, j * 512:(j + 1) * 512]:
                            v.tensor_copy(o, s), inc="tV")
            done_read(bks, ev)
            t_ost[p] = dma("act", out[tt * 128:(tt + 1) * 128, :], ost[p][:, :],
                           deps=[ev])

        for semn in POOL["sync"]:
            B.ops["sync"].append(("w", semn, B.cnt[semn]))
        for semn in POOL["gp"]:
            B.ops["gp"].append(("w", semn, B.cnt[semn]))
        for semn in POOL["act"]:
            B.ops["act"].append(("w", semn, B.cnt[semn]))

        B.replay(block, semh)
    return nc


def make_consts():
    cbf = np.zeros((128, 160), dtype=ml_dtypes.bfloat16)
    cbf[:, 0:128] = np.eye(128, dtype=np.float32).astype(ml_dtypes.bfloat16)
    cbf[:, 128:160] = np.ones((128, 32), np.float32).astype(ml_dtypes.bfloat16)
    cf32 = np.zeros((128, 256), dtype=np.float32)
    cf32[:, 0:128] = np.eye(128, dtype=np.float32)
    cf32[:, 128:256] = 1.0
    ci32 = np.full((128, 8), MAGIC, dtype=np.int32)
    return cbf, cf32, ci32


def kernel(L0_attn_x, L0_attn_w_qkv, L0_attn_w_o, L0_ffn_W_up, L0_ffn_W_gate,
           L0_ffn_W_down):
    x = np.ascontiguousarray(np.asarray(L0_attn_x, dtype=np.float32))
    ncores, TOK, _ = x.shape
    nc = build_nc(TOK)
    cbf, cf32, ci32 = make_consts()
    base = {
        "wqkv": np.ascontiguousarray(np.asarray(L0_attn_w_qkv, dtype=np.float32)),
        "wo": np.ascontiguousarray(np.asarray(L0_attn_w_o, dtype=np.float32)),
        "wup": np.ascontiguousarray(np.asarray(L0_ffn_W_up, dtype=np.float32)),
        "wgate": np.ascontiguousarray(np.asarray(L0_ffn_W_gate, dtype=np.float32)),
        "wdown": np.ascontiguousarray(np.asarray(L0_ffn_W_down, dtype=np.float32)),
        "cbf": cbf, "cf32": cf32, "ci32": ci32,
    }
    in_maps = [dict(base, x=x[b]) for b in range(ncores)]
    res = run_bass_kernel_spmd(nc, in_maps, core_ids=list(range(ncores)))
    return np.stack([res.results[b]["out"] for b in range(ncores)], axis=0)


# version 16
# speedup vs baseline: 1.0700x; 1.0700x over previous
import numpy as np
import ml_dtypes
import concourse.bass as bass
import concourse.mybir as mybir
from concourse.bass_utils import run_bass_kernel_spmd
from contextlib import ExitStack

F32 = mybir.dt.float32
BF16 = mybir.dt.bfloat16
I32 = mybir.dt.int32
AF = mybir.ActivationFunctionType
ALU = mybir.AluOpType
AX = mybir.AxisListType

D = 2048
E = 8192
H = 16
F3 = 6144
DC = D // 128
EC = E // 128
ISQ = 0.08838834764831845
LN_EPS = 1e-5
Q_EPS = 1e-8
MAGIC = 0x5F3759DF
AR = 105000

ENGS = ("sync", "pe", "vec", "act", "gp")
NPOOL_S = 14
NPOOL_G = 6
NPOOL_A = 10
POOL = {"sync": ["qS%d" % i for i in range(NPOOL_S)],
        "gp": ["qG%d" % i for i in range(NPOOL_G)],
        "act": ["qA%d" % i for i in range(NPOOL_A)]}
SEMS = (tuple(POOL["sync"]) + tuple(POOL["gp"]) + tuple(POOL["act"])
        + ("tT", "tV", "tS", "tG"))


class Build:
    def __init__(self):
        self.ops = {e: [] for e in ENGS}
        self.cnt = {s: 0 for s in SEMS}
        self.hw = {e: {} for e in ENGS}

    def wait(self, e, tok):
        if tok is None:
            return
        sem, val = tok
        if val <= 0 or self.hw[e].get(sem, 0) >= val:
            return
        self.hw[e][sem] = val
        self.ops[e].append(("w", sem, val))

    def waits(self, e, toks):
        for t in toks:
            self.wait(e, t)

    def emit(self, e, fn, inc=None, k=1):
        tok = None
        if inc is not None:
            self.cnt[inc] += k
            tok = (inc, self.cnt[inc])
        self.ops[e].append(("i", fn, inc, k))
        return tok

    def barrier(self):
        snap = dict(self.cnt)
        for e in ENGS:
            for s, v in snap.items():
                self.wait(e, (s, v))

    def replay(self, block, semh):
        def mk(entries):
            def body(eng):
                for op in entries:
                    if op[0] == "w":
                        eng.wait_ge(semh[op[1]], op[2])
                    else:
                        inst = op[1](eng)
                        if op[2] is not None:
                            inst.then_inc(semh[op[2]], op[3])
            return body
        block.sync(mk(self.ops["sync"]))
        block.tensor(mk(self.ops["pe"]))
        block.vector(mk(self.ops["vec"]))
        block.scalar(mk(self.ops["act"]))
        block.gpsimd(mk(self.ops["gp"]))


def build_nc(TOK, upto=None):
    _seen = []

    def _go(name):
        if upto is not None and upto in _seen:
            return False
        _seen.append(name)
        return True

    NT = TOK // 128
    NTC = TOK // 512
    NBLK = TOK // 512
    NKC = TOK // 512
    NKQ = max(1, TOK // 1024)
    KQW = min(1024, TOK)
    NUQ = max(1, TOK // 1024)
    UW = min(1024, TOK)

    nc = bass.Bass("TRN2", target_bir_lowering=False)

    x_in = nc.dram_tensor("x", [TOK, D], F32, kind="ExternalInput").ap()
    wqkv = nc.dram_tensor("wqkv", [D, F3], F32, kind="ExternalInput").ap()
    wo = nc.dram_tensor("wo", [D, D], F32, kind="ExternalInput").ap()
    wup = nc.dram_tensor("wup", [E, D], F32, kind="ExternalInput").ap()
    wgate = nc.dram_tensor("wgate", [E, D], F32, kind="ExternalInput").ap()
    wdown = nc.dram_tensor("wdown", [D, E], F32, kind="ExternalInput").ap()
    cbf = nc.dram_tensor("cbf", [128, 160], BF16, kind="ExternalInput").ap()
    cf32 = nc.dram_tensor("cf32", [128, 256], F32, kind="ExternalInput").ap()
    ci32 = nc.dram_tensor("ci32", [128, 8], I32, kind="ExternalInput").ap()
    out = nc.dram_tensor("out", [TOK, D], F32, kind="ExternalOutput").ap()

    s_x0 = nc.dram_tensor("s_x0", [DC, 128, TOK], F32).ap()
    s_x1 = nc.dram_tensor("s_x1", [DC, 128, TOK], F32).ap()
    s_x2 = nc.dram_tensor("s_x2", [DC, 128, TOK], F32).ap()
    s_x3 = nc.dram_tensor("s_x3", [DC, 128, TOK], F32).ap()
    s_xln = nc.dram_tensor("s_xln", [NTC, 128, DC * 512], BF16).ap()
    s_tug = nc.dram_tensor("s_tug", [EC, 128, 4096], BF16).ap()
    s_td = nc.dram_tensor("s_td", [DC, 128, EC * 128], BF16).ap()
    s_cv = nc.dram_tensor("s_cv", [NTC, 128, H * 512], BF16).ap()

    B = Build()

    with ExitStack() as ctx:
        sbt = lambda n, s, d: ctx.enter_context(nc.sbuf_tensor(n, s, d))
        c_bf = sbt("c_bf", [128, 160], BF16)
        c_f = sbt("c_f", [128, 256], F32)
        c_i = sbt("c_i", [128, 8], I32)
        sm = sbt("sm", [128, 64], F32)
        arena = sbt("arena", [128, AR], BF16)
        ps = ctx.enter_context(nc.psum_tensor("ps", [128, 4096], F32))
        semh = {n: ctx.enter_context(nc.semaphore(n)) for n in SEMS}
        block = ctx.enter_context(nc.Block())

        ident_b = c_bf[:, 0:128]
        ones_b = c_bf[:, 128:160]
        ident_f = c_f[:, 0:128]
        ones_f = c_f[:, 128:256]

        def av(off, n, dt=BF16, p0=0, p1=128):
            nel = n if dt == BF16 else 2 * n
            assert off + nel <= AR, (off, n, dt)
            if dt == BF16:
                return arena[p0:p1, off:off + n]
            assert off % 2 == 0
            return arena[p0:p1, off:off + 2 * n].bitcast(dt)

        def bank(i, w=1):
            return ps[:, 512 * i:512 * (i + w)]

        SM_ACC = 0
        SM_TMP = 3
        SM_TH = 4
        SM_RS = 10
        SM_Z = 12
        SM_ZS, SM_IZ, SM_BIAS, SM_QM, SM_KM = 36, 39, 42, 43, 44
        SM_SC = 46

        bk_w = [None] * 8
        bk_r = [[] for _ in range(8)]

        def mm_group(banks_used, fns):
            for b in banks_used:
                B.waits("pe", bk_r[b])
            t = None
            for i, f in enumerate(fns):
                t = B.emit("pe", f, inc="tT" if i == len(fns) - 1 else None)
            for b in banks_used:
                bk_w[b] = t
                bk_r[b] = []
            return t

        def done_read(banks_used, t):
            for b in banks_used:
                bk_r[b].append(t)

        pool_idx = {"sync": 0, "gp": 0, "act": 0}

        def dma(e, dst, src, deps=()):
            pool = POOL[e]
            i = pool_idx[e] % len(pool)
            pool_idx[e] += 1
            sem = pool[i]
            B.wait(e, (sem, B.cnt[sem]))
            B.waits(e, deps)
            return B.emit(e, lambda eng, d=dst, s=src: eng.dma_start(d, s),
                          inc=sem, k=16)

        def vop(fn, waits=()):
            B.waits("vec", waits)
            return B.emit("vec", fn, inc="tV")

        def vchain(fns, waits=()):
            B.waits("vec", waits)
            t = None
            for f in fns:
                if t is not None:
                    B.wait("vec", t)
                t = B.emit("vec", f, inc="tV")
            return t

        def rowv(base, slot):
            return av(base + slot * 2 * TOK, TOK, F32, 0, 1)

        def rowtmp(base, which):
            return av(base + 4 * TOK + 1024 * which, 512, F32, 0, 1)

        def strip_mms(bA, bB, src_b, src_q, first, last):
            t = B.emit("pe", lambda t_, o=ps[0:32, 512 * bA:512 * bA + 512], s=src_b,
                       st=first, sp=last: t_.matmul(o, ones_b, s, start=st, stop=sp))
            t = B.emit("pe", lambda t_, o=ps[0:32, 512 * bB:512 * bB + 512], s=src_q,
                       st=first, sp=last: t_.matmul(o, ones_b, s, start=st, stop=sp),
                       inc="tT")
            return t

        def strip_finalize_tc(tcx, tok, bA, bB, rows_off):
            rS = rowtmp(rows_off, 0)
            rQ = rowtmp(rows_off, 1)
            rT = rowtmp(rows_off, 2)
            rM = rowv(rows_off, 0)[0:1, tcx * 512:tcx * 512 + 512]
            rR = rowv(rows_off, 1)[0:1, tcx * 512:tcx * 512 + 512]
            ev = vop(lambda v: v.tensor_copy(rS, ps[0:1, 512 * bA:512 * bA + 512]),
                     waits=[tok])
            ev = vop(lambda v: v.tensor_copy(rQ, ps[0:1, 512 * bB:512 * bB + 512]),
                     waits=[ev])
            done_read([bA], ev)
            done_read([bB], ev)
            seq = [
                lambda v: v.tensor_scalar(rM, rS, 1.0 / D, None, ALU.mult),
                lambda v: v.tensor_scalar(rQ, rQ, 1.0 / D, None, ALU.mult),
                lambda v: v.tensor_tensor(rT, rM, rM, ALU.mult),
                lambda v: v.tensor_tensor(rQ, rQ, rT, ALU.subtract),
                lambda v: v.tensor_scalar(rQ, rQ, 1.0, LN_EPS, ALU.mult, ALU.add),
                lambda v: v.tensor_scalar(rR.bitcast(I32), rQ.bitcast(I32), 1, None,
                                          ALU.arith_shift_right),
                lambda v: v.tensor_tensor(rR.bitcast(I32),
                                          c_i[0:1, 0:1].broadcast_to([1, 512]),
                                          rR.bitcast(I32), ALU.subtract),
            ]
            for _ in range(4):
                seq += [
                    lambda v: v.tensor_tensor(rT, rQ, rR, ALU.mult),
                    lambda v: v.tensor_tensor(rT, rT, rR, ALU.mult),
                    lambda v: v.tensor_scalar(rT, rT, -0.5, 1.5, ALU.mult, ALU.add),
                    lambda v: v.tensor_tensor(rR, rR, rT, ALU.mult),
                ]
            return vchain(seq, waits=[ev])

        tc_ = [dma("sync", c_bf[:, :], cbf), dma("sync", c_f[:, :], cf32),
               dma("sync", c_i[:, :], ci32)]
        t_sm0 = vop(lambda v: v.memset(sm[:, :], 0.0), waits=tc_)
        for _e in ("pe", "act", "gp"):
            B.waits(_e, tc_ + [t_sm0])

        xin = [av(0, 2048, F32), av(8192, 2048, F32)]
        x0st = [av(16384, 2048, F32), av(24576, 2048, F32)]
        p0_last_ev = None
        t_xin_cons = [None, None]
        t_x0dma = [None, None]
        for tt in range(NT):
            p = tt % 2
            tl = dma("sync", xin[p][:, :], x_in[tt * 128:(tt + 1) * 128, :],
                     deps=[t_xin_cons[p]])
            bks = [0, 1, 2, 3] if p == 0 else [4, 5, 6, 7]
            B.wait("pe", tl)
            fns = []
            for dc in range(DC):
                b = bks[dc // 4]
                off = 512 * b + (dc % 4) * 128
                fns.append(lambda t_, s=xin[p][:, dc * 128:(dc + 1) * 128],
                           o=ps[:, off:off + 128], st=(dc % 4 == 0), sp=(dc % 4 == 3):
                           t_.matmul(o, s, ident_f, start=st, stop=sp))
            tmm = mm_group(bks, fns)
            t_xin_cons[p] = tmm
            ev = None
            B.waits("vec", [tmm, t_x0dma[p]])
            for j in range(4):
                ev = B.emit("vec", lambda v, s=bank(bks[j]),
                            o=x0st[p][:, j * 512:(j + 1) * 512]:
                            v.tensor_copy(o, s), inc="tV")
            done_read(bks, ev)
            t_x0dma[p] = dma("act",
                             s_x0[:, :, tt * 128:(tt + 1) * 128].transpose([1, 0, 2]),
                             x0st[p][:, :].rearrange("p (c f) -> p c f", c=16),
                             deps=[ev])
            p0_last_ev = ev

        qw = [av(65536, 8192, F32), av(81920, 8192, F32)]
        qa = av(32768, 8192, BF16)
        qacc = av(50000, 160, F32)
        t_qw_cons = [None, None]
        slast = None
        ti = 0
        tile_ranges = []
        for mi, (W, ntile, cols) in enumerate([(wup, 64, 2048), (wgate, 64, 2048),
                                               (wdown, 16, 8192)]):
            tile_ranges.append((ti, ti + ntile))
            for i in range(ntile):
                p = i % 2
                tl = dma("act", qw[p][:, 0:cols], W[i * 128:(i + 1) * 128, :],
                         deps=[t_qw_cons[p]])
                B.waits("act", [tl, slast, t_sm0])
                t = B.emit("act", lambda s_, src=qw[p][:, 0:cols], n=cols, j=ti:
                           s_.activation(qa[:, 0:n], src, AF.Abs,
                                         accum_out=qacc[:, j:j + 1]), inc="tS")
                slast = t
                t_qw_cons[p] = t
                ti += 1
        vlast = None
        for mi in range(3):
            lo, hi = tile_ranges[mi]
            vlast = vop(lambda v, a=SM_ACC + mi, lo=lo, hi=hi:
                        v.reduce_sum(sm[:, a:a + 1], qacc[:, lo:hi], axis=AX.X),
                        waits=[slast, vlast])
        ROWS1 = 30000
        p0_done = [t_xin_cons[0], t_xin_cons[1], p0_last_ev,
                   t_x0dma[0], t_x0dma[1]]
        xt1 = [av(98304 + i * 1024, 512, F32) for i in range(3)]
        xb1 = [av(101376, 512, BF16), av(101888, 512, BF16)]
        xq1 = [av(102400, 512, BF16), av(102912, 512, BF16)]
        t_x_cons = [None] * 3
        t_bq_cons = [None, None]
        pend_strip = None
        rows1_evs = []
        for tcx in range(NTC):
            for dc in range(DC):
                p3 = dc % 3
                p = dc % 2
                tl = dma("sync", xt1[p3][:, :], s_x0[dc, :, tcx * 512:(tcx + 1) * 512],
                         deps=[t_x_cons[p3]] + (p0_done if tcx == 0 and dc < 3 else []))
                tb = vop(lambda v, o=xb1[p][:, :], s=xt1[p3][:, :]:
                         v.tensor_copy(o, s), waits=[tl, t_bq_cons[p]])
                t_x_cons[p3] = tb
                B.waits("act", [tb, t_bq_cons[p]])
                tq = B.emit("act", lambda s, o=xq1[p][:, :], i_=xb1[p][:, :]:
                            s.activation(o, i_, AF.Square), inc="tS")
                if pend_strip is not None:
                    pdc, ptb, ptq, ppx = pend_strip
                    B.waits("pe", [ptb, ptq])
                    if pdc == 0:
                        B.waits("pe", bk_r[6] + bk_r[7])
                    tstr = strip_mms(6, 7, xb1[ppx][:, :], xq1[ppx][:, :],
                                     pdc == 0, pdc == DC - 1)
                    t_bq_cons[ppx] = tstr
                    if pdc == DC - 1:
                        bk_w[6] = bk_w[7] = tstr
                        bk_r[6] = []
                        bk_r[7] = []
                        rows1_evs.append(strip_finalize_tc(tcx - 1, tstr, 6, 7, ROWS1))
                pend_strip = (dc, tb, tq, p)
        pdc, ptb, ptq, ppx = pend_strip
        B.waits("pe", [ptb, ptq])
        tstr = strip_mms(6, 7, xb1[ppx][:, :], xq1[ppx][:, :], pdc == 0,
                         pdc == DC - 1)
        t_bq_cons[ppx] = tstr
        bk_w[6] = bk_w[7] = tstr
        bk_r[6] = []
        bk_r[7] = []
        rows1_evs.append(strip_finalize_tc(NTC - 1, tstr, 6, 7, ROWS1))
        rows1_tok = rows1_evs[-1]

        B.wait("pe", vlast)
        fns = [lambda t_, jj=j, a=SM_ACC + j:
               t_.matmul(ps[0:1, jj:jj + 1], sm[:, a:a + 1], ones_f[:, 0:1],
                         start=(jj == 0), stop=(jj == 2)) for j in range(3)]
        tsc = mm_group([0], fns)
        ev = vop(lambda v: v.tensor_copy(sm[0:1, SM_SC:SM_SC + 3], ps[0:1, 0:3]),
                 waits=[tsc])
        done_read([0], ev)
        B.wait("pe", ev)
        fns = [lambda t_, jj=j: t_.matmul(ps[:, 512 + jj:512 + jj + 1],
                                          ones_f[0:1, 0:128],
                                          sm[0:1, SM_SC + jj:SM_SC + jj + 1],
                                          start=(jj == 0), stop=(jj == 2))
               for j in range(3)]
        tbc = mm_group([1], fns)
        cnts = [float(E * D), float(E * D), float(D * E)]
        t = None
        for j in range(3):
            t = vchain([
                lambda v, jj=j, c=cnts[j]:
                v.tensor_scalar(sm[:, SM_TH + 2 * jj:SM_TH + 2 * jj + 1],
                                ps[:, 512 + jj:512 + jj + 1],
                                0.5 / c, 0.5 * Q_EPS, ALU.mult, ALU.add),
                lambda v, jj=j:
                v.tensor_scalar(sm[:, SM_TH + 2 * jj + 1:SM_TH + 2 * jj + 2],
                                sm[:, SM_TH + 2 * jj:SM_TH + 2 * jj + 1],
                                -1.0, None, ALU.mult),
            ], waits=[tbc, t])
        done_read([1], t)
        t_scales = vchain([
            lambda v: v.tensor_tensor(sm[:, SM_RS:SM_RS + 1],
                                      sm[:, SM_TH + 4:SM_TH + 5],
                                      sm[:, SM_TH + 2:SM_TH + 3], ALU.mult),
            lambda v: v.tensor_scalar(sm[:, SM_RS:SM_RS + 1],
                                      sm[:, SM_RS:SM_RS + 1], 0.5, None, ALU.mult),
        ], waits=[t])

        def ln_normalize(src_x, rows_off, rows_tok, dst_dram, dst_off, extra_deps,
                         scr=0):
            LNT = dst_off + DC * TOK if dst_dram is None else scr
            xt = [av(LNT, 512, F32), av(LNT + 1024, 512, F32)]
            tmp = av(LNT + 2048, 512, F32)
            xo = [av(LNT + 3072, 512, BF16), av(LNT + 3584, 512, BF16)]
            rM = rowv(rows_off, 0)
            rR = rowv(rows_off, 1)
            B.wait("pe", rows_tok)
            t_x_cons = [None, None]
            t_o_cons = [None, None]
            out_toks = []
            tlast = None
            for tc in range(NTC):
                t0 = mm_group([0], [lambda t_, s=rM[0:1, tc * 512:tc * 512 + 512]:
                                    t_.matmul(bank(0), ones_f[0:1, 0:128], s,
                                              start=True, stop=True)])
                t1 = mm_group([1], [lambda t_, s=rR[0:1, tc * 512:tc * 512 + 512]:
                                    t_.matmul(bank(1), ones_f[0:1, 0:128], s,
                                              start=True, stop=True)])
                tv = None
                for dc in range(DC):
                    p = dc % 2
                    tl = dma("sync", xt[p][:, :], src_x[dc, :, tc * 512:(tc + 1) * 512],
                             deps=[t_x_cons[p], rows_tok]
                             + (extra_deps if tc == 0 and dc < 2 else []))
                    dst = (xo[p][:, :] if dst_dram is not None
                           else av(dst_off + dc * TOK + tc * 512, 512, BF16))
                    tv = vchain([
                        lambda v, s=xt[p][:, :]:
                        v.tensor_tensor(tmp, s, bank(0), ALU.subtract),
                        lambda v, o=dst: v.tensor_tensor(o, tmp, bank(1), ALU.mult),
                    ], waits=[tl, t0, t1, t_o_cons[p], tlast])
                    t_x_cons[p] = tv
                    tlast = tv
                    if dst_dram is not None:
                        t_o_cons[p] = dma("act",
                                          dst_dram[tc][:, dc * 512:(dc + 1) * 512],
                                          xo[p][:, :], deps=[tv])
                        out_toks.append(t_o_cons[p])
                done_read([0], tv)
                done_read([1], tv)
            if dst_dram is not None:
                return out_toks
            return [tlast]

        xln1_done = ln_normalize(s_x0, ROWS1, rows1_tok, s_xln, 0, [], scr=98304)
        _go("ln1")

        B.waits("act", p0_done)
        s1b = [av(0, 8192, BF16), av(8192, 8192, BF16)]
        s2b = [av(16384, 8192, BF16), av(24576, 8192, BF16)]
        t2b = [av(32768, 8192, BF16), av(40960, 8192, BF16)]
        tst = [av(49152, 8192, BF16), av(57344, 8192, BF16)]
        t_sgn_gp = [None, None]
        t_t2_cons = [None, None]
        t_tst_cons = [[], []]
        B.waits("act", [t_scales])
        for W, SOFF, ntile, cols, thj in ((wup, 0, 64, 2048, 0),
                                          (wgate, 2048, 64, 2048, 1),
                                          (wdown, None, 16, 8192, 2)):
            nth_c = SM_TH + 2 * thj + 1
            th_c = SM_TH + 2 * thj
            for i in range(ntile):
                p = i % 2
                tl = dma("act", qw[p][:, 0:cols], W[i * 128:(i + 1) * 128, :],
                         deps=[t_qw_cons[p]])
                B.waits("act", [tl, t_sgn_gp[p]])
                ts1 = B.emit("act", lambda s, o=s1b[p], src=qw[p][:, 0:cols],
                             c=nth_c, n=cols:
                             s.activation(o[:, 0:n], src, AF.Sign,
                                          bias=sm[:, c:c + 1]), inc="tS")
                ts2 = B.emit("act", lambda s, o=s2b[p], src=qw[p][:, 0:cols],
                             c=th_c, n=cols:
                             s.activation(o[:, 0:n], src, AF.Sign,
                                          bias=sm[:, c:c + 1]), inc="tS")
                t_qw_cons[p] = ts2
                tgp = vop(lambda v, n=cols, pp=p:
                          v.tensor_add(t2b[pp][:, 0:n], s1b[pp][:, 0:n],
                                       s2b[pp][:, 0:n]),
                          waits=[ts1, ts2, t_t2_cons[p]])
                t_sgn_gp[p] = tgp
                B.wait("pe", tgp)
                ngrp = cols // 512
                evtok = None
                tmm = None
                for g in range(ngrp):
                    b = 2 + (g % 2)
                    fns = []
                    for cc in range(4):
                        c = g * 4 + cc
                        fns.append(lambda t_, src=t2b[p][:, c * 128:(c + 1) * 128],
                                   o=ps[:, 512 * b + cc * 128:512 * b + (cc + 1) * 128],
                                   st=(cc == 0), sp=(cc == 3):
                                   t_.matmul(o, src, ident_b, start=st, stop=sp))
                    tmm = mm_group([b], fns)
                    evtok = vop(lambda v, s=bank(b), o=tst[p][:, g * 512:(g + 1) * 512]:
                                v.tensor_copy(o, s), waits=[tmm] + (t_tst_cons[p] or []))
                    done_read([b], evtok)
                t_t2_cons[p] = tmm
                if cols == 2048:
                    t_tst_cons[p] = [dma("sync", s_tug[i][:, SOFF:SOFF + 2048],
                                         tst[p][:, 0:2048], deps=[evtok])]
                else:
                    t_tst_cons[p] = [dma("sync", s_td[i], tst[p][:, 0:8192],
                                         deps=[evtok])]
        B.barrier()

        def ffn(xln_ready, src_x, dst_x, rows_off):
            hid = av(0, 32768, BF16)
            xlnb = av(32768, 8192, BF16)
            stu = [av(40960 + 4096 * i, 4096, BF16) for i in range(3)]
            tdb = [av(53248 + 8192 * i, 8192, BF16) for i in range(3)]
            x0r = [av(77824, 512, F32), av(78848, 512, F32)]
            x1s = [av(79872, 512, F32), av(80896, 512, F32)]
            silu = [av(81920, 512, BF16), av(82432, 512, BF16)]
            x1b = [av(82944, 512, BF16), av(83456, 512, BF16)]
            x1q = [av(83968, 512, BF16), av(84480, 512, BF16)]
            t_xlnb_cons = None
            t_stu_cons = [None] * 3
            t_td_cons = [None] * 3
            t_x0_cons = [None, None]
            t_x1_cons = [None, None]
            t_sil_cons = [None, None]
            t_bq_cons = [None, None]
            pend_strip = None
            hid_tok = {}
            rows_evs = []
            outtoks = []
            want_stats = rows_off is not None
            for blk in range(NBLK):
                t_xlnb = dma("sync", xlnb[:, :], s_xln[blk],
                             deps=[t_xlnb_cons] + (xln_ready if blk == 0 else []))
                for e in range(EC):
                    pe_ = e % 3
                    tsu = dma("sync", stu[pe_][:, :], s_tug[e],
                              deps=[t_stu_cons[pe_]])
                    bu, bg = (0, 1) if e % 2 == 0 else (2, 3)
                    B.waits("pe", [tsu, t_xlnb])
                    fns = [lambda t_, s=stu[pe_][:, d * 128:(d + 1) * 128],
                           m=xlnb[:, d * 512:(d + 1) * 512], o=bank(bu),
                           st=(d == 0), sp=(d == DC - 1):
                           t_.matmul(o, s, m, start=st, stop=sp) for d in range(DC)]
                    tu_mm = mm_group([bu], fns)
                    t_stu_cons[pe_] = tu_mm
                    fns = [lambda t_, s=stu[pe_][:, 2048 + d * 128:2048 + (d + 1) * 128],
                           m=xlnb[:, d * 512:(d + 1) * 512], o=bank(bg),
                           st=(d == 0), sp=(d == DC - 1):
                           t_.matmul(o, s, m, start=st, stop=sp) for d in range(DC)]
                    tg_mm = mm_group([bg], fns)
                    t_stu_cons[pe_] = tg_mm
                    if e == EC - 1:
                        t_xlnb_cons = tg_mm
                    sl_ = e % 2
                    B.waits("act", [tu_mm, t_sil_cons[sl_]])
                    tsl = B.emit("act", lambda s, o=silu[sl_][:, :], i_=bank(bu):
                                 s.activation(o, i_, AF.Silu,
                                              scale=sm[:, SM_TH:SM_TH + 1]), inc="tS")
                    done_read([bu], tsl)
                    thd = vop(lambda v, o=hid[:, e * 512:(e + 1) * 512],
                              a=silu[sl_][:, :], b=bank(bg):
                              v.tensor_tensor(o, a, b, ALU.mult),
                              waits=[tsl, tg_mm])
                    done_read([bg], thd)
                    t_sil_cons[sl_] = thd
                    hid_tok[e] = thd
                tstr = None
                for dc in range(DC):
                    pd = dc % 3
                    px = dc % 2
                    ttd = dma("sync", tdb[pd][:, :], s_td[dc], deps=[t_td_cons[pd]])
                    tx0 = dma("act", x0r[px][:, :],
                              src_x[dc, :, blk * 512:(blk + 1) * 512],
                              deps=[t_x0_cons[px]])
                    bd = 4 + px
                    B.wait("pe", ttd)
                    if dc == 0:
                        B.waits("pe", [hid_tok[e] for e in range(EC)])
                    fns = [lambda t_, s=tdb[pd][:, e * 128:(e + 1) * 128],
                           m=hid[:, e * 512:(e + 1) * 512], o=bank(bd),
                           st=(e == 0), sp=(e == EC - 1):
                           t_.matmul(o, s, m, start=st, stop=sp) for e in range(EC)]
                    td_mm = mm_group([bd], fns)
                    t_td_cons[pd] = td_mm
                    tev = vop(lambda v, o=x1s[px][:, :], i_=bank(bd), x_=x0r[px][:, :]:
                              v.scalar_tensor_tensor(o, i_, sm[:, SM_RS:SM_RS + 1],
                                                     x_, ALU.mult, ALU.add),
                              waits=[td_mm, tx0, t_x1_cons[px]])
                    done_read([bd], tev)
                    t_x0_cons[px] = tev
                    tod = dma("act", dst_x[dc, :, blk * 512:(blk + 1) * 512],
                              x1s[px][:, :], deps=[tev])
                    outtoks.append(tod)
                    t_x1_cons[px] = tod
                    if want_stats:
                        tb = vop(lambda v, o=x1b[px][:, :], s=x1s[px][:, :]:
                                 v.tensor_copy(o, s), waits=[tev, t_bq_cons[px]])
                        B.waits("act", [tb, t_bq_cons[px]])
                        tq = B.emit("act", lambda s, o=x1q[px][:, :], i_=x1b[px][:, :]:
                                    s.activation(o, i_, AF.Square), inc="tS")
                        if pend_strip is not None:
                            pdc, ptb, ptq, ppx = pend_strip
                            B.waits("pe", [ptb, ptq])
                            if pdc == 0:
                                B.waits("pe", bk_r[6] + bk_r[7])
                            tstr = strip_mms(6, 7, x1b[ppx][:, :], x1q[ppx][:, :],
                                             pdc == 0, pdc == DC - 1)
                            t_bq_cons[ppx] = tstr
                        pend_strip = (dc, tb, tq, px)
                if want_stats:
                    pdc, ptb, ptq, ppx = pend_strip
                    B.waits("pe", [ptb, ptq])
                    tstr = strip_mms(6, 7, x1b[ppx][:, :], x1q[ppx][:, :],
                                     pdc == 0, pdc == DC - 1)
                    t_bq_cons[ppx] = tstr
                    pend_strip = None
                    bk_w[6] = bk_w[7] = tstr
                    bk_r[6] = []
                    bk_r[7] = []
                    rows_evs.append(strip_finalize_tc(blk, tstr, 6, 7, rows_off))
            rt = rows_evs[-1] if want_stats else None
            return rt, outtoks

        ROWSF = 85504
        if _go("ffn1"):
            rows2_tok, x1toks = ffn(xln1_done, s_x0, s_x1, ROWSF)

        XLN2 = 0
        if _go("ln2"):
            B.barrier()
            xln2_done = ln_normalize(s_x1, ROWSF, rows2_tok, None, XLN2, x1toks)

        _do_attn = _go("attn_a")
        if not _do_attn:
            xln2_done = []
        B.barrier()
        A0 = DC * TOK + 4096 + 64
        qb = av(A0, TOK, BF16)
        kb = av(A0 + TOK, TOK, BF16)
        vb = av(A0 + 2 * TOK, TOK, BF16)
        cvb = av(A0 + 4 * TOK + 6144, TOK, BF16)
        e0p = [av(A0 + 3 * TOK, TOK, BF16), cvb]
        kb2 = av(A0 + 5 * TOK + 6272 + 4096, TOK, BF16)
        kbufs = [kb, kb2]
        wqh = [av(A0 + 4 * TOK + 2048 * i, 2048, BF16) for i in range(3)]
        wrep = av(A0 + 5 * TOK + 6144, 96, BF16)
        kmb = av(A0 + 5 * TOK + 6240, 2, BF16)
        CSTG = A0 + 5 * TOK + 6272
        cstg = av(CSTG, 2048, F32)

        B.waits("pe", xln2_done)
        t_wqh_cons = [None] * 3
        t_cv_dma = None
        cv_toks = []
        t_e0_cons = [None, None]
        t_cstg_cons = []
        nr0 = 32 * min(4, NKC)
        nr1 = 32 * max(0, NKC - 4)
        wrepp = [wrep[:, 32 * i:32 * i + 32] for i in range(3)]

        def load_wqh(h):
            toks = []
            for ci in range(3):
                src = wqkv[:, ci * D + h * 128:ci * D + (h + 1) * 128]
                toks.append(dma("gp", wqh[ci][:, :].rearrange("p (c f) -> p c f",
                                                              c=16),
                                src.rearrange("(c p) f -> p c f", p=128),
                                deps=[t_wqh_cons[ci]]))
            return toks

        def qkv_units(h, twq, cis):
            evs = {}
            for ci in cis:
                for th in range(NUQ):
                    nb = UW // 512
                    b0 = 2 * (th % 2) if ci == 2 else 0
                    bks = list(range(b0, b0 + nb))
                    B.wait("pe", twq[ci])
                    fns = []
                    for d in range(DC):
                        for nn in range(nb):
                            fns.append(lambda t_,
                                       s=wqh[ci][:, d * 128:(d + 1) * 128],
                                       m=av(XLN2 + d * TOK + th * UW + nn * 512, 512),
                                       o=bank(b0 + nn), st=(d == 0),
                                       sp=(d == DC - 1):
                                       t_.matmul(o, s, m, start=st, stop=sp))
                    tmm = mm_group(bks, fns)
                    if th == NUQ - 1:
                        t_wqh_cons[ci] = tmm
                    dstb = (qb, kb, vb)[ci]
                    if ci == 2:
                        ev = vop(lambda v, o=dstb[:, th * UW:(th + 1) * UW],
                                 s=ps[:, 512 * b0:512 * b0 + UW]:
                                 v.tensor_copy(o, s), waits=[tmm, t_cv_dma])
                    else:
                        B.waits("act", [tmm])
                        ev = B.emit("act", lambda s_, o=dstb[:, th * UW:(th + 1) * UW],
                                    s=ps[:, 0:UW]:
                                    s_.copy(o, s), inc="tS")
                    done_read(bks, ev)
                evs[ci] = ev
            return evs

        def sq_strips(buf):
            fns = []
            for kc in range(NKC):
                bb = kc // 4
                jj = kc % 4
                fns.append(lambda t_, s=buf[:, kc * 512:(kc + 1) * 512],
                           o=ps[32 * jj:32 * jj + 32, 512 * bb:512 * bb + 512],
                           tp=(0, 32 * jj):
                           t_.matmul(o, ones_b, s, start=True, stop=True,
                                     tile_position=tp))
            return mm_group([0, 1] if nr1 else [0], fns)

        def strip_max(dst_col, tok):
            seq = [lambda v: v.reduce_max(sm[0:nr0, dst_col:dst_col + 1],
                                          ps[0:nr0, 0:512], axis=AX.X)]
            if nr1:
                seq += [
                    lambda v: v.reduce_max(sm[0:nr1, SM_TMP:SM_TMP + 1],
                                           ps[0:nr1, 512:1024], axis=AX.X),
                    lambda v: v.tensor_tensor(sm[:, dst_col:dst_col + 1],
                                              sm[:, dst_col:dst_col + 1],
                                              sm[:, SM_TMP:SM_TMP + 1], ALU.max),
                ]
            return vchain(seq, waits=[tok])

        def norms(evq, evk, kbv, wait_exp=None):
            tqs = vop(lambda v: v.tensor_tensor(e0p[0][:, 0:TOK], qb[:, 0:TOK],
                                                qb[:, 0:TOK], ALU.mult),
                      waits=[evq, t_e0_cons[0]])
            B.wait("pe", tqs)
            tq_str = sq_strips(e0p[0])
            tqm = strip_max(SM_QM, tq_str)
            done_read([0], tqm)
            done_read([1], tqm)
            tks = vop(lambda v: v.tensor_tensor(e0p[0][:, 0:TOK], kbv[:, 0:TOK],
                                                kbv[:, 0:TOK], ALU.mult),
                      waits=[evk, tq_str])
            B.wait("pe", tks)
            tk_str = sq_strips(e0p[0])
            t_e0_cons[0] = tk_str
            tkm = strip_max(SM_KM, tk_str)
            done_read([0], tkm)
            done_read([1], tkm)
            t = vop(lambda v: v.tensor_copy(kmb[:, 0:2], sm[:, SM_QM:SM_QM + 2]),
                    waits=[tqm, tkm])
            B.wait("pe", t)
            tt_ = mm_group([2], [
                lambda t_: t_.matmul(ps[0:1, 1024:1152], kmb[:, 0:1], ident_b,
                                     start=True, stop=False),
                lambda t_: t_.matmul(ps[0:1, 1152:1280], kmb[:, 1:2], ident_b,
                                     start=False, stop=True),
            ])
            t = vchain([
                lambda v: v.reduce_max(sm[0:1, SM_QM:SM_QM + 1], ps[0:1, 1024:1152],
                                       axis=AX.X),
                lambda v: v.reduce_max(sm[0:1, SM_KM:SM_KM + 1], ps[0:1, 1152:1280],
                                       axis=AX.X),
                lambda v: v.tensor_tensor(sm[0:1, SM_BIAS:SM_BIAS + 1],
                                          sm[0:1, SM_QM:SM_QM + 1],
                                          sm[0:1, SM_KM:SM_KM + 1], ALU.add),
                lambda v: v.tensor_scalar(sm[0:1, SM_BIAS:SM_BIAS + 1],
                                          sm[0:1, SM_BIAS:SM_BIAS + 1],
                                          -0.505 * ISQ, None, ALU.mult),
            ], waits=[tt_])
            done_read([2], t)
            B.waits("pe", [t] + bk_r[2])
            tt_ = mm_group([2], [lambda t_: t_.matmul(ps[:, 1024:1025],
                                                      ones_f[0:1, 0:128],
                                                      sm[0:1, SM_BIAS:SM_BIAS + 1],
                                                      start=True, stop=True)])
            tbias = vop(lambda v: v.tensor_copy(sm[:, SM_BIAS:SM_BIAS + 1],
                                                ps[:, 1024:1025]),
                        waits=[tt_, wait_exp])
            done_read([2], tbias)
            return tbias

        CB = [6, 6, 6, 6, 7, 7, 7, 7]
        CBANKS = sorted(set(CB[:NKC]))

        def cpos(kc):
            return CB[kc], 32 * (kc % 4)

        def emit_cmms(qt, pq, e0b, t):
            B.wait("pe", t)
            tcm = None
            for kc in range(NKC):
                bb, jj = cpos(kc)
                if qt == 0 and kc == 0:
                    for b_ in CBANKS:
                        B.waits("pe", bk_r[b_])
                tcm = B.emit("pe", lambda t_, m=e0b[:, kc * 512:(kc + 1) * 512],
                             w=wrepp[pq],
                             o=ps[jj:jj + 32, 512 * bb:512 * bb + 512],
                             st=(qt == 0), sp=(qt == NT - 1), tp=(0, jj):
                             t_.matmul(o, w, m, start=st, stop=sp,
                                       tile_position=tp),
                             inc="tT" if kc == NKC - 1 else None)
            return tcm

        def scores_loop(tbias, kbv, fillers, sc_toks):
            pend = []
            tcm = None
            pre = None

            def sc_group(qt, kq):
                sc_bks = (2, 3) if (qt + kq) % 2 == 0 else (4, 5)
                nn2 = KQW // 512
                fns = [lambda t_, s=qb[:, qt * 128:(qt + 1) * 128],
                       m=kbv[:, kq * 1024 + nn * 512:kq * 1024 + (nn + 1) * 512],
                       o=bank(sc_bks[nn]):
                       t_.matmul(o, s, m, start=True, stop=True)
                       for nn in range(nn2)]
                return mm_group(list(sc_bks[:nn2]), fns), sc_bks, nn2

            for qt in range(NT):
                pq = qt % 2
                e0b = e0p[pq]
                te = None
                tvz = None
                for kq in range(NKQ):
                    if kq == 0 and pre is not None:
                        tmm, sc_bks, nn2 = pre
                    else:
                        tmm, sc_bks, nn2 = sc_group(qt, kq)
                    sc_toks[qt] = tmm
                    B.waits("act", [tmm, tbias, t_e0_cons[pq]])
                    if kq < NKQ // 2:
                        te = B.emit("act",
                                    lambda s_, o=e0b[:, kq * 1024:kq * 1024 + KQW],
                                    i_=ps[:, 512 * sc_bks[0]:512 * sc_bks[0] + KQW]:
                                    s_.activation(o, i_, AF.Exp, scale=ISQ,
                                                  bias=sm[:, SM_BIAS:SM_BIAS + 1]),
                                    inc="tS")
                        tvz = vop(lambda v, s=e0b[:, kq * 1024:kq * 1024 + KQW],
                                  z=sm[:, SM_Z + 8 * pq + kq:SM_Z + 8 * pq + kq + 1]:
                                  v.reduce_sum(z, s, axis=AX.X),
                                  waits=[te, tvz])
                    else:
                        te = B.emit("act", lambda s_,
                                    o=e0b[:, kq * 1024:kq * 1024 + KQW],
                                    i_=ps[:, 512 * sc_bks[0]:512 * sc_bks[0] + KQW],
                                    z=sm[:, SM_Z + 8 * pq + kq:SM_Z + 8 * pq + kq + 1]:
                                    s_.activation(o, i_, AF.Exp, scale=ISQ,
                                                  bias=sm[:, SM_BIAS:SM_BIAS + 1],
                                                  accum_out=z), inc="tS")
                    done_read(list(sc_bks[:nn2]), te)
                t = vchain([
                    lambda v, n=NKQ, o=8 * pq, pq=pq: v.reduce_sum(
                        sm[:, SM_ZS + pq:SM_ZS + pq + 1],
                        sm[:, SM_Z + o:SM_Z + o + n], axis=AX.X),
                    lambda v, pq=pq: v.reciprocal(sm[:, SM_IZ + pq:SM_IZ + pq + 1],
                                                  sm[:, SM_ZS + pq:SM_ZS + pq + 1]),
                    lambda v, w=wrepp[pq], pq=pq: v.tensor_copy(
                        w, sm[:, SM_IZ + pq:SM_IZ + pq + 1].broadcast_to([128, 32])),
                ], waits=[te, tvz, t_e0_cons[pq]])
                pend.append((qt, pq, e0b, t))
                if len(pend) > 1:
                    pqt, ppq, pe0b, pt = pend.pop(0)
                    tcm = emit_cmms(pqt, ppq, pe0b, pt)
                    t_e0_cons[ppq] = tcm
                for f in fillers.get(qt, []):
                    f()
                pre = sc_group(qt + 1, 0) if qt + 1 < NT else None
            for pqt, ppq, pe0b, pt in pend:
                tcm = emit_cmms(pqt, ppq, pe0b, pt)
                t_e0_cons[ppq] = tcm
            return tcm, te

        def cbcast_part1(cstrip_tok):
            ev = None
            for gi, b_ in enumerate(CBANKS):
                ev = vop(lambda v, o=cstg[:, gi * 512:(gi + 1) * 512], s=bank(b_):
                         v.tensor_copy(o, s),
                         waits=[cstrip_tok, ev] + ([] if gi else (t_cstg_cons or [])))
                done_read([b_], ev)
            fix = []
            for kc in range(NKC):
                if 32 * (kc % 4) == 96:
                    gi = CBANKS.index(CB[kc])
                    fix.append(dma("gp",
                                   cstg[0:1, (2 + kc // 4) * 512:
                                             (3 + kc // 4) * 512],
                                   cstg[96:97, gi * 512:(gi + 1) * 512],
                                   deps=[ev]))
            return ev, fix

        def cbcast_part2(h, ev, fix, evv):
            t_cvw = None
            cstg_rd = []
            for kc in range(NKC):
                bb = 2 + kc % 2
                _, jj = cpos(kc)
                gi = CBANKS.index(CB[kc])
                if jj == 96:
                    r0, c0 = 0, (2 + kc // 4) * 512
                    B.waits("pe", [ev, fix[kc // 4]])
                else:
                    r0, c0 = jj, gi * 512
                    B.wait("pe", ev)
                tt_ = mm_group([bb], [
                    lambda t_, o=bank(bb), w=ones_f[r0:r0 + 1, 0:128],
                    s=cstg[r0:r0 + 1, c0:c0 + 512]:
                    t_.matmul(o, w, s, start=True, stop=True)])
                cstg_rd.append(tt_)
                t_cvw = vop(lambda v, o=cvb[:, kc * 512:(kc + 1) * 512],
                            a=vb[:, kc * 512:(kc + 1) * 512], b=bank(bb):
                            v.tensor_tensor(o, a, b, ALU.mult),
                            waits=[tt_, evv, t_cv_dma] +
                                  ([t_e0_cons[1]] if kc == 0 else []))
                done_read([bb], t_cvw)
            td = dma("act", s_cv[:, :, h * 512:(h + 1) * 512].transpose([1, 0, 2]),
                     cvb[:, 0:TOK].rearrange("p (c f) -> p c f", c=NTC),
                     deps=[t_cvw])
            return td, cstg_rd

        def mk_fillers(hn, sc_toks, twq):
            fillers = {}
            kdst = kbufs[hn % 2]
            ev_last = {}
            st = {}
            units = []
            for th in range(NUQ):
                units.append((1, th))
                units.append((0, th))
            for u, (ci, th) in enumerate(units):
                for s4 in range(4):
                    slot = 4 * u + s4

                    def sub(ci=ci, th=th, s4=s4):
                        if s4 == 0:
                            B.waits("pe", bk_r[0])
                            B.waits("pe", bk_r[1])
                            B.wait("pe", twq[ci])
                        last = None
                        for dd in range(4):
                            d = s4 * 4 + dd
                            for nn in range(2):
                                fin = (s4 == 3 and dd == 3 and nn == 1)
                                last = B.emit(
                                    "pe", lambda t_,
                                    s=wqh[ci][:, d * 128:(d + 1) * 128],
                                    m=av(XLN2 + d * TOK + th * UW + nn * 512,
                                         512),
                                    o=bank(nn), st_=(d == 0), sp=(d == DC - 1):
                                    t_.matmul(o, s, m, start=st_, stop=sp),
                                    inc="tT" if fin else None)
                        if s4 == 3:
                            st[(ci, th)] = last
                            for b in (0, 1):
                                bk_w[b] = last
                                bk_r[b] = []
                    fillers.setdefault(slot, []).append(sub)

                def evac(ci=ci, th=th):
                    tmm = st[(ci, th)]
                    if th == NUQ - 1:
                        t_wqh_cons[ci] = tmm
                    dstb = qb if ci == 0 else kdst
                    w = [tmm]
                    if ci == 0:
                        w.append(sc_toks[8 * th + 7])
                    ev = vop(lambda v, o=dstb[:, th * UW:(th + 1) * UW],
                             s=ps[:, 0:UW]: v.tensor_copy(o, s), waits=w)
                    done_read([0, 1], ev)
                    ev_last[ci] = ev
                fillers.setdefault(4 * u + 3, []).append(evac)
            return fillers, ev_last

        twq = load_wqh(0)
        if _do_attn:
            evs = qkv_units(0, twq, (0, 1, 2))
            evq, evk, evv = evs[0], evs[1], evs[2]
            tbias = norms(evq, evk, kb)
        last_te = None
        for h in range(H if _do_attn else 0):
            sc_toks = {}
            if h + 1 < H:
                twq = load_wqh(h + 1)
                fillers, ev_last = mk_fillers(h + 1, sc_toks, twq)
            else:
                fillers = {}
            cstrip_tok, last_te = scores_loop(tbias, kbufs[h % 2], fillers,
                                              sc_toks)
            for b_ in CBANKS:
                bk_w[b_] = cstrip_tok
                bk_r[b_] = []
            myevv = evv
            ev1, fix1 = cbcast_part1(cstrip_tok)
            if h + 1 < H:
                tbias = norms(ev_last[0], ev_last[1], kbufs[(h + 1) % 2],
                              last_te)
            t_cv_dma, t_cstg_cons = cbcast_part2(h, ev1, fix1, myevv)
            t_e0_cons[1] = t_cv_dma
            cv_toks.append(t_cv_dma)
            if h + 1 < H:
                evs = qkv_units(h + 1, twq, (2,))
                evv = evs[2]

        _do_attnb = _go("attn_b")
        B.barrier()
        ROWS3 = 60000
        wob = av(0, 32768, BF16)
        cvt = [av(32768, 8192, BF16), av(40960, 8192, BF16)]
        x1r = [av(49152 + 1024 * i, 512, F32) for i in range(4)]
        x2b = [av(53248, 512, BF16), av(53760, 512, BF16)]
        x2q = [av(54272, 512, BF16), av(54784, 512, BF16)]
        X2K = 80000
        x2k = [av(X2K + 1024 * dc, 512, F32) for dc in range(DC)]
        lntmp = av(X2K + 16384, 512, F32)
        lnout = [av(X2K + 17408, 512, BF16), av(X2K + 17920, 512, BF16)]
        two_toks = []
        for fc in range(DC if _do_attnb else 0):
            two_toks.append(dma("gp", wob[:, fc * 2048:(fc + 1) * 2048],
                                wo[fc * 128:(fc + 1) * 128, :], deps=[]))
        t_cvt_cons = [None, None]
        t_x1_cons = [None] * 4
        t_x2_cons = {}
        t_bq_cons = [None, None]
        t_lnout_cons = [None, None]
        rM3 = rowv(ROWS3, 0)
        rR3 = rowv(ROWS3, 1)
        pend_strip = None
        pend_ln = None
        rows3_evs = []
        x2toks = []
        xln3_toks = []

        ln_prev = [None]

        def ln3_dc(tcx, dc, t0, t1):
            p = dc % 2
            tv = vchain([
                lambda v, s=x2k[dc][:, :]:
                v.tensor_tensor(lntmp, s, ps[:, 2048:2560], ALU.subtract),
                lambda v, o=lnout[p][:, :]:
                v.tensor_tensor(o, lntmp, ps[:, 2560:3072], ALU.mult),
            ], waits=[t0, t1, t_lnout_cons[p], ln_prev[0]])
            ln_prev[0] = tv
            tok = dma("gp", s_xln[tcx][:, dc * 512:(dc + 1) * 512],
                      lnout[p][:, :], deps=[tv])
            t_lnout_cons[p] = tok
            xln3_toks.append(tok)
            return tv
        for tcx in range(NTC if _do_attnb else 0):
            pt = tcx % 2
            tcv = dma("sync", cvt[pt][:, :], s_cv[tcx],
                      deps=[t_cvt_cons[pt]] + (cv_toks if tcx < 2 else []))
            tstr = None
            for dc in range(DC):
                pd = dc % 2
                p4 = dc % 4
                bo = pd
                if pend_ln is not None:
                    tvln = ln3_dc(pend_ln[0], dc, pend_ln[1], pend_ln[2])
                    if dc == DC - 1:
                        done_read([4], tvln)
                        done_read([5], tvln)
                        pend_ln = None
                B.waits("pe", two_toks + [tcv])
                fns = [lambda t_,
                       s=wob[:, hh * 2048 + dc * 128:hh * 2048 + (dc + 1) * 128],
                       m=cvt[pt][:, hh * 512:(hh + 1) * 512], o=bank(bo),
                       st=(hh == 0), sp=(hh == H - 1):
                       t_.matmul(o, s, m, start=st, stop=sp) for hh in range(H)]
                tmm = mm_group([bo], fns)
                if dc >= DC - 2:
                    t_cvt_cons[pt] = tmm
                tx1 = dma("sync", x1r[p4][:, :],
                          s_x1[dc, :, tcx * 512:(tcx + 1) * 512],
                          deps=[t_x1_cons[p4]])
                tev = vop(lambda v, o=x2k[dc][:, :], i_=bank(bo),
                          x_=x1r[p4][:, :]:
                          v.tensor_tensor(o, i_, x_, ALU.add),
                          waits=[tmm, tx1, t_x2_cons.get(dc)])
                done_read([bo], tev)
                t_x1_cons[p4] = tev
                tod = dma("act", s_x2[dc, :, tcx * 512:(tcx + 1) * 512],
                          x2k[dc][:, :], deps=[tev])
                x2toks.append(tod)
                t_x2_cons[dc] = tod
                tb = vop(lambda v, o=x2b[pd][:, :], s=x2k[dc][:, :]:
                         v.tensor_copy(o, s), waits=[tev, t_bq_cons[pd]])
                B.waits("act", [tb, t_bq_cons[pd]])
                tq = B.emit("act", lambda s, o=x2q[pd][:, :], i_=x2b[pd][:, :]:
                            s.activation(o, i_, AF.Square), inc="tS")
                if pend_strip is not None:
                    pdc, ptb, ptq, ppx = pend_strip
                    B.waits("pe", [ptb, ptq])
                    if pdc == 0:
                        B.waits("pe", bk_r[2] + bk_r[3])
                    tstr = strip_mms(2, 3, x2b[ppx][:, :], x2q[ppx][:, :],
                                     pdc == 0, pdc == DC - 1)
                    t_bq_cons[ppx] = tstr
                pend_strip = (dc, tb, tq, pd)
            pdc, ptb, ptq, ppx = pend_strip
            B.waits("pe", [ptb, ptq])
            tstr = strip_mms(2, 3, x2b[ppx][:, :], x2q[ppx][:, :],
                             pdc == 0, pdc == DC - 1)
            t_bq_cons[ppx] = tstr
            pend_strip = None
            bk_w[2] = bk_w[3] = tstr
            bk_r[2] = []
            bk_r[3] = []
            rows3_evs.append(strip_finalize_tc(tcx, tstr, 2, 3, ROWS3))
            B.waits("pe", [rows3_evs[-1]] + bk_r[4] + bk_r[5])
            t0 = mm_group([4], [lambda t_, s=rM3[0:1, tcx * 512:tcx * 512 + 512]:
                                t_.matmul(ps[:, 2048:2560], ones_f[0:1, 0:128],
                                          s, start=True, stop=True)])
            t1 = mm_group([5], [lambda t_, s=rR3[0:1, tcx * 512:tcx * 512 + 512]:
                                t_.matmul(ps[:, 2560:3072], ones_f[0:1, 0:128],
                                          s, start=True, stop=True)])
            pend_ln = (tcx, t0, t1)
        if _do_attnb:
            rows3_tok = rows3_evs[-1]
            ltcx, t0, t1 = pend_ln
            tvln = None
            for dc in range(DC):
                tvln = ln3_dc(ltcx, dc, t0, t1)
            done_read([4], tvln)
            done_read([5], tvln)
            pend_ln = None

        if _go("ln3"):
            xln3_done = xln3_toks
        if _go("ffn2"):
            B.barrier()
            _, x3toks = ffn(xln3_done, s_x2, s_x3, None)

        _do_final = _go("final")
        B.barrier()
        x3r = [av(4096 * i, 2048, F32) for i in range(3)]
        ost = [av(16384 + 4096 * i, 2048, F32) for i in range(3)]
        t_in_cons = [None, None, None]
        t_ost = [None, None, None]
        for tt in range(NT if _do_final else 0):
            p = tt % 3
            tl = dma("sync", x3r[p][:, :].rearrange("p (c f) -> p c f", c=16),
                     s_x3[:, :, tt * 128:(tt + 1) * 128].transpose([1, 0, 2]),
                     deps=[t_in_cons[p]] + (x3toks if tt < 3 else []))
            bks = [0, 1, 2, 3] if tt % 2 == 0 else [4, 5, 6, 7]
            B.wait("pe", tl)
            fns = []
            for dc in range(DC):
                b = bks[dc // 4]
                off = 512 * b + (dc % 4) * 128
                fns.append(lambda t_, s=x3r[p][:, dc * 128:(dc + 1) * 128],
                           o=ps[:, off:off + 128], st=(dc % 4 == 0),
                           sp=(dc % 4 == 3):
                           t_.matmul(o, s, ident_f, start=st, stop=sp))
            tmm = mm_group(bks, fns)
            t_in_cons[p] = tmm
            ev = None
            B.waits("vec", [tmm, t_ost[p]])
            for j in range(4):
                ev = B.emit("vec", lambda v, s=bank(bks[j]),
                            o=ost[p][:, j * 512:(j + 1) * 512]:
                            v.tensor_copy(o, s), inc="tV")
            done_read(bks, ev)
            t_ost[p] = dma("act", out[tt * 128:(tt + 1) * 128, :], ost[p][:, :],
                           deps=[ev])

        for semn in POOL["sync"]:
            B.ops["sync"].append(("w", semn, B.cnt[semn]))
        for semn in POOL["gp"]:
            B.ops["gp"].append(("w", semn, B.cnt[semn]))
        for semn in POOL["act"]:
            B.ops["act"].append(("w", semn, B.cnt[semn]))

        B.replay(block, semh)
    return nc


def make_consts():
    cbf = np.zeros((128, 160), dtype=ml_dtypes.bfloat16)
    cbf[:, 0:128] = np.eye(128, dtype=np.float32).astype(ml_dtypes.bfloat16)
    cbf[:, 128:160] = np.ones((128, 32), np.float32).astype(ml_dtypes.bfloat16)
    cf32 = np.zeros((128, 256), dtype=np.float32)
    cf32[:, 0:128] = np.eye(128, dtype=np.float32)
    cf32[:, 128:256] = 1.0
    ci32 = np.full((128, 8), MAGIC, dtype=np.int32)
    return cbf, cf32, ci32


def kernel(L0_attn_x, L0_attn_w_qkv, L0_attn_w_o, L0_ffn_W_up, L0_ffn_W_gate,
           L0_ffn_W_down):
    x = np.ascontiguousarray(np.asarray(L0_attn_x, dtype=np.float32))
    ncores, TOK, _ = x.shape
    nc = build_nc(TOK)
    cbf, cf32, ci32 = make_consts()
    base = {
        "wqkv": np.ascontiguousarray(np.asarray(L0_attn_w_qkv, dtype=np.float32)),
        "wo": np.ascontiguousarray(np.asarray(L0_attn_w_o, dtype=np.float32)),
        "wup": np.ascontiguousarray(np.asarray(L0_ffn_W_up, dtype=np.float32)),
        "wgate": np.ascontiguousarray(np.asarray(L0_ffn_W_gate, dtype=np.float32)),
        "wdown": np.ascontiguousarray(np.asarray(L0_ffn_W_down, dtype=np.float32)),
        "cbf": cbf, "cf32": cf32, "ci32": ci32,
    }
    in_maps = [dict(base, x=x[b]) for b in range(ncores)]
    res = run_bass_kernel_spmd(nc, in_maps, core_ids=list(range(ncores)))
    return np.stack([res.results[b]["out"] for b in range(ncores)], axis=0)
